# version 02
# baseline (speedup 1.0000x reference)
import contextlib
import os
import numpy as np
import concourse.bass as bass
import concourse.mybir as mybir
from concourse.bass_utils import run_bass_kernel_spmd

F32 = mybir.dt.float32
BF16 = mybir.dt.bfloat16
AF = mybir.ActivationFunctionType
ALU = mybir.AluOpType
AX = mybir.AxisListType
ts = bass.ts

H = 4096
NH = 32
B = 2
S = 4096
KT = 32
NSP = 14
ITL = 8
NIT = 112
NOQ = 8
SCALE = float(1.0 / np.sqrt(128.0))
EPS = 1e-5
NT = 40


def umin(kt):
    return kt // 4


KTN = [16, 32]
SP_N = [[], []]
SP_S0 = [[], []]
SP_OFF = [[], []]
for _hf in range(2):
    _off = 0
    for _kt in range(KTN[_hf]):
        _lo = max(umin(_kt), 4 * _hf)
        _n = (4 * _hf + 4 - _lo) * 128
        SP_N[_hf].append(_n)
        SP_S0[_hf].append((_lo - 4 * _hf) * 128)
        SP_OFF[_hf].append(_off)
        _off += _n
ECOLS = max(sum(SP_N[0]), sum(SP_N[1]))

PAIR_IDX = {}
_pi = 0
for _kt in range(KT):
    for _u in range(umin(_kt), 8):
        PAIR_IDX[(_kt, _u)] = _pi
        _pi += 1
NPAIR = _pi

ENGS = ("sp", "gp", "pe", "act", "dve")


class Em:

    def __init__(self):
        self.emit = False
        self.only = None
        self.count = {}
        self.marks = {}
        self.stream = {e: [] for e in ENGS}
        self.eng = {e: None for e in ENGS}
        self.semh = {}
        self.wfloor = {}

    def mval(self, sem, key):
        if isinstance(key, int):
            return key
        return self.marks.get((sem, key), 0)

    def op(self, e, build, waits=(), inc=None, tag=""):
        if not self.emit:
            if inc is not None:
                sem, k, mk = inc
                self.count[sem] = self.count.get(sem, 0) + k
                if mk is not None:
                    self.marks[(sem, mk)] = self.count[sem]
            self.stream[e].append((waits, inc, tag))
            return
        if e != self.only:
            return
        eng = self.eng[e]
        floor = self.wfloor.setdefault(e, {})
        for sem, key in waits:
            val = self.mval(sem, key)
            if val > 0 and floor.get(sem, 0) < val:
                eng.wait_ge(self.semh[sem], val)
                floor[sem] = val
        instr = build(eng) if build is not None else None
        if inc is not None:
            sem, k, mk = inc
            instr.then_inc(self.semh[sem], k)

    def check(self):
        pc = {e: 0 for e in ENGS}
        cnt = {}
        progress = True
        while progress:
            progress = False
            for e in ENGS:
                while pc[e] < len(self.stream[e]):
                    waits, inc, tag = self.stream[e][pc[e]]
                    if any(cnt.get(s, 0) < self.mval(s, k) for s, k in waits):
                        break
                    if inc is not None:
                        cnt[inc[0]] = cnt.get(inc[0], 0) + inc[1]
                    pc[e] += 1
                    progress = True
        stuck = {}
        for e in ENGS:
            if pc[e] < len(self.stream[e]):
                w, _, tag = self.stream[e][pc[e]]
                stuck[e] = (pc[e], tag,
                            [(s, self.mval(s, k), cnt.get(s, 0)) for s, k in w])
        if stuck:
            raise RuntimeError(f"schedule deadlock: {stuck}")


def build_phases(em, D):
    ITLE = int(os.environ.get("KITL", str(ITL)))
    NSPE = int(os.environ.get("KNSP", str(NSP)))
    NOQE = int(os.environ.get("KNOQ", str(NOQ)))
    P2MODE = int(os.environ.get("KP2M", "9"))

    def li(hf, h):
        return hf * NH + h

    def ktn_of(L):
        return KTN[L // NH] if L >= 0 else KTN[0]

    def dinc(stream, idx):
        return (f"d_{stream}{idx % 2}", 16, (stream, idx))

    def dwait(stream, idx):
        return (f"d_{stream}{idx % 2}", (stream, idx))

    def dinc6(stream, idx):
        return (f"d_{stream}{idx % 6}", 16, (stream, idx))

    def dwait6(stream, idx):
        return (f"d_{stream}{idx % 6}", (stream, idx))

    BAR0 = [0]

    def bar_wait(e, ph):
        em.op(e, None, waits=[("bar", 5 * ph - BAR0[0])], tag=f"barw{ph}")

    def bar_inc(e, ph):
        em.op(e, lambda g: g.nop(), inc=("bar", 1, None), tag=f"bari{ph}")

    def p0a_sp():
        e = "sp"
        em.op(e, lambda g: g.dma_start(out=D["ID32"][:, :], in_=D["ident"]),
              inc=("d_msc", 16, None))
        em.op(e, lambda g: g.dma_start(out=D["ONR"][:, :], in_=D["ones_row"]),
              inc=("d_msc", 16, None))
        em.op(e, lambda g: g.dma_start(out=D["EPSC"][:, :], in_=D["epsc"]),
              inc=("d_msc", 16, None))
        for tt in range(NT):
            src = D["xk"][tt] if tt < 32 else D["xq"][tt - 32]
            em.op(e, lambda g, s=src, b=tt % 2: g.dma_start(
                out=D["XKB"][b][:, :], in_=s),
                waits=[("c_dve", ("dst", tt - 2)), ("c_act", ("ast", tt - 2))],
                inc=dinc("xk", tt), tag=f"xk{tt}")
        for ht in range(KT):
            em.op(e, lambda g, h=ht: g.dma_start(
                out=D["XQTB"][h % 2][:, :], in_=D["xqT"][h]),
                waits=[("c_dve", ("xnt", ht - 2))],
                inc=dinc("xqt", ht), tag=f"xqt{ht}")

    def p0a_gp():
        e = "gp"
        casts = [("WKV", "wkv"), ("ID16", "ident"), ("ON16", "ones_col"),
                 ("MSK", "masks")]
        for dst, src in casts:
            em.op(e, lambda g, d=dst, s=src: g.dma_start(out=D[d][:, :],
                                                         in_=D[s]),
                  inc=("d_cst", 16, None), tag=f"cst{dst}")

    CST_ALL = 64

    def p0a_dve():
        e = "dve"
        for tt in range(NT):
            em.op(e, lambda g, t=tt: g.reduce_sum(
                D["SUMS"][:, t:t + 1], D["XKB"][t % 2][:, :], axis=AX.X),
                waits=[dwait("xk", tt)],
                inc=("c_dve", 1, ("dst", tt)), tag=f"red{tt}")
        em.op(e, lambda g: g.tensor_scalar(
            D["MNEG"][:, :], D["SUMS"][:, :], -1.0 / 4096.0, None, ALU.mult),
            waits=[("c_dve", ("dst", NT - 1))],
            inc=("c_dve", 1, ("mneg",)), tag="mneg")
        em.op(e, lambda g: g.tensor_scalar(
            D["EX2"][:, :], D["SUMSQ"][:, :], 1.0 / 4096.0, None, ALU.mult),
            waits=[("c_act", ("ast", NT - 1))],
            inc=("c_dve", 1, ("ex2",)), tag="ex2")
        em.op(e, lambda g: g.tensor_tensor(
            D["MSQ"][:, :], D["MNEG"][:, :], D["MNEG"][:, :], ALU.mult),
            waits=[("c_dve", ("mneg",))],
            inc=("c_dve", 1, ("msq",)), tag="msq")
        em.op(e, lambda g: g.tensor_tensor(
            D["EX2"][:, :], D["EX2"][:, :], D["MSQ"][:, :], ALU.subtract),
            waits=[("c_dve", ("msq",)), ("c_dve", ("ex2",))],
            inc=("c_dve", 1, ("var",)), tag="var")
        em.op(e, lambda g: g.tensor_tensor(
            D["MR_OWN"][0:1, :], D["MRO_OWN"][0:1, :], D["RR_OWN"][0:1, :],
            ALU.mult),
            waits=[("c_act", ("strev", 15))],
            inc=("c_dve", 1, ("mrown",)), tag="mrown")
        for ht in range(KT):
            em.op(e, lambda g, h=ht: g.tensor_tensor(
                D["SCR"][:, 0:1024], D["XQTB"][h % 2][:, :], D["RBCO"][:, :],
                ALU.mult),
                waits=[dwait("xqt", ht), ("c_act", ("obc", 3)),
                       ("c_act", ("ast", NT - 1)),
                       ("c_dve", ("xnt", ht - 1))],
                inc=("c_dve", 1, ("xnm", ht)), tag=f"xnm{ht}")
            em.op(e, lambda g, h=ht: g.tensor_tensor(
                D["XNT"][:, ts(h, 1024)], D["SCR"][:, 0:1024],
                D["MRBCO"][:, :], ALU.add),
                waits=[("c_dve", ("xnm", ht))],
                inc=("c_dve", 1, ("xnt", ht)), tag=f"xns{ht}")

    def p0a_act():
        e = "act"
        for tt in range(NT):
            em.op(e, lambda g, t=tt: g.activation(
                D["SCR"][:, 0:4096], D["XKB"][t % 2][:, :], AF.Square,
                accum_out=D["SUMSQ"][:, t:t + 1]),
                waits=[dwait("xk", tt), ("c_act", ("ast", tt - 1))],
                inc=("c_act", 1, ("ast", tt)), tag=f"sq{tt}")
        em.op(e, lambda g: g.activation(D["LNV"][:, :], D["EX2"][:, :],
                                        AF.Ln, bias=D["EPSC"][:, 0:1]),
              waits=[("c_dve", ("var",)), ("d_msc", 48)],
              inc=("c_act", 1, ("lnv",)), tag="lnv")
        em.op(e, lambda g: g.activation(D["RSTD"][:, :], D["LNV"][:, :],
                                        AF.Exp, scale=-0.5),
              waits=[("c_act", ("lnv",))],
              inc=("c_act", 1, ("rstd",)), tag="rstd")
        for w in range(2):
            em.op(e, lambda g, ww=w: g.copy(D["CSB"][:, ww:ww + 1],
                                            D["CS"][:, ww:ww + 1]),
                  waits=[("c_pe", ("csum", w))],
                  inc=("c_act", 1, ("csev", w)), tag=f"csev{w}")
        for c in range(32, NT):
            for w in range(2):
                idx = 2 * (c - 32) + w
                dst = D["MRO_OWN"] if w == 0 else D["RR_OWN"]
                ofs = 128 * (c - 32)
                em.op(e, lambda g, d=dst, o=ofs: g.copy(
                    d[0:1, o:o + 128], D["TR"][0:1, :]),
                    waits=[("c_pe", ("str", idx))],
                    inc=("c_act", 1, ("strev", idx)), tag=f"strev{idx}")
        for w in range(2):
            dst = D["CKR"] if w == 0 else D["CVR"]
            em.op(e, lambda g, d=dst: g.copy(d[0:1, :], D["TR"][0:1, :]),
                  waits=[("c_pe", ("cstr", w))],
                  inc=("c_act", 1, ("cstrev", w)), tag=f"cstrev{w}")
        for i in range(4):
            dst = D["RBCO"] if i < 2 else D["MRBCO"]
            em.op(e, lambda g, d=dst, ii=i: g.copy(
                d[:, 512 * (ii % 2):512 * (ii % 2) + 512], D["RB"][:, :]),
                waits=[("c_pe", ("obc", i))],
                inc=("c_act", 1, ("obc", i)), tag=f"obcev{i}")

    def p0a_pe():
        e = "pe"
        for w in range(2):
            for kt in range(KT):
                em.op(e, lambda g, ww=w, k=kt: g.matmul(
                    D["CS"][:, ww:ww + 1],
                    D["WKV"][:, k * 256 + 128 * ww:k * 256 + 128 * ww + 128],
                    D["ON16"][:, :], start=(k == 0), stop=(k == KT - 1)),
                    waits=([("d_cst", CST_ALL)] if (w == 0 and kt == 0)
                           else ([("c_act", ("csev", 0))]
                                 if (w == 1 and kt == 0) else [])),
                    inc=(("c_pe", 1, ("csum", w)) if kt == KT - 1 else None),
                    tag=f"cs{w}_{kt}")
        for c in range(32, NT):
            for w in range(2):
                idx = 2 * (c - 32) + w
                srcT = D["MNEG"] if w == 0 else D["RSTD"]
                wl = [("c_act", ("strev", idx - 1))]
                if idx == 0:
                    wl += [("c_dve", ("var",)), ("d_msc", 48)]
                if w == 1:
                    wl.append(("c_act", ("rstd",)))
                em.op(e, lambda g, s=srcT, cc=c: g.transpose(
                    D["TR"][:, :], s[:, cc:cc + 1], D["ID32"][:, :]),
                    waits=wl, inc=("c_pe", 1, ("str", idx)), tag=f"str{idx}")
        for w in range(2):
            wl = [("c_act", ("csev", w)), ("c_act", ("strev", 15))]
            if w == 1:
                wl.append(("c_act", ("cstrev", 0)))
            em.op(e, lambda g, ww=w: g.transpose(
                D["TR"][:, :], D["CSB"][:, ww:ww + 1], D["ID32"][:, :]),
                waits=wl, inc=("c_pe", 1, ("cstr", w)), tag=f"cstr{w}")
        for i in range(4):
            row = "RR_OWN" if i < 2 else "MR_OWN"
            wl = [("c_act", ("cstrev", 1)), ("c_act", ("obc", i - 1))]
            if i == 2:
                wl.append(("c_dve", ("mrown",)))
            em.op(e, lambda g, r=row, ii=i: g.matmul(
                D["RB"][:, :], D["ONR"][:, :],
                D[r][0:1, 512 * (ii % 2):512 * (ii % 2) + 512],
                start=True, stop=True),
                waits=wl, inc=("c_pe", 1, ("obc", i)), tag=f"obc{i}")

    def p0b_gp():
        e = "gp"
        for ch in range(8):
            for ht in range(KT):
                em.op(e, lambda g, c=ch, h=ht: g.dma_start(
                    out=D["SLAB"][c % 2][:, ts(h, 512)],
                    in_=D["xkT"][h][:, 512 * c:512 * c + 512]),
                    waits=([("c_pe", ("kv", ch - 2))] if ht == 0 else []),
                    inc=(dinc("slab", ch) if ht == KT - 1
                         else (f"d_slab{ch % 2}", 16, None)),
                    tag=f"slab{ch}_{ht}")

    def p0b_pe():
        e = "pe"
        for ch in range(8):
            for l in range(4):
                for w in range(2):
                    kidx = ch * 8 + l * 2 + w
                    srcT = D["MNEG"] if w == 0 else D["RSTD"]
                    em.op(e, lambda g, s=srcT, cc=4 * ch + l: g.transpose(
                        D["TR"][:, :], s[:, cc:cc + 1], D["ID32"][:, :]),
                        waits=[("c_act", ("krowev", kidx - 1))],
                        inc=("c_pe", 1, ("krow", kidx)), tag=f"krow{kidx}")
            em.op(e, lambda g, c=ch: g.matmul(
                D["RB"][:, :], D["ONR"][:, :],
                D["RRB"][c % 2][0:1, :], start=True, stop=True),
                waits=[("c_act", ("rbev", ch - 1)),
                       ("c_dve", ("kvev", ch - 2)),
                       ("c_act", ("krowev", ch * 8 + 7))],
                inc=("c_pe", 1, ("rb", ch)), tag=f"rb{ch}")
            for kt in range(KT):
                wl = []
                if kt == 0:
                    wl = [dwait("slab", ch), ("c_dve", ("kvev", ch - 2))]
                em.op(e, lambda g, c=ch, k=kt: g.matmul(
                    D["KP"][c % 2][:, :], D["WKV"][:, k * 256:k * 256 + 128],
                    D["SLAB"][c % 2][:, ts(k, 512)],
                    start=(k == 0), stop=False),
                    waits=wl, tag=f"kvK{ch}_{kt}")
                em.op(e, lambda g, c=ch, k=kt: g.matmul(
                    D["VP"][c % 2][:, :],
                    D["WKV"][:, k * 256 + 128:k * 256 + 256],
                    D["SLAB"][c % 2][:, ts(k, 512)],
                    start=(k == 0), stop=False), tag=f"kvV{ch}_{kt}")
            em.op(e, lambda g, c=ch: g.matmul(
                D["KP"][c % 2][:, :], D["CKR"][0:1, :],
                D["MRB"][c % 2][0:1, :], start=False, stop=True),
                tag=f"kvKo{ch}")
            em.op(e, lambda g, c=ch: g.matmul(
                D["VP"][c % 2][:, :], D["CVR"][0:1, :],
                D["MRB"][c % 2][0:1, :], start=False, stop=True),
                inc=("c_pe", 1, ("kv", ch)), tag=f"kvVo{ch}")
            if ch >= 1:
                for l in range(4):
                    vi = (ch - 1) * 4 + l
                    em.op(e, lambda g, c=ch - 1, ll=l: g.transpose(
                        D["PVT"][:, :], D["VTTB"][c % 2][:, ts(ll, 128)],
                        D["ID16"][:, :]),
                        waits=[("c_dve", ("kvev", ch - 1)),
                               ("c_dve", ("vtev", vi - 1))],
                        inc=("c_pe", 1, ("vt", vi)), tag=f"vt{vi}")
        for l in range(4):
            vi = 7 * 4 + l
            em.op(e, lambda g, ll=l: g.transpose(
                D["PVT"][:, :], D["VTTB"][1][:, ts(ll, 128)], D["ID16"][:, :]),
                waits=[("c_dve", ("kvev", 7)), ("c_dve", ("vtev", vi - 1))],
                inc=("c_pe", 1, ("vt", vi)), tag=f"vt{vi}")

    def p0b_act():
        e = "act"
        for ch in range(8):
            for l in range(4):
                for w in range(2):
                    kidx = ch * 8 + l * 2 + w
                    dst = D["MRB"] if w == 0 else D["RRB"]
                    em.op(e, lambda g, d=dst, c=ch, ll=l: g.copy(
                        d[c % 2][0:1, 128 * ll:128 * ll + 128],
                        D["TR"][0:1, :]),
                        waits=[("c_pe", ("krow", kidx))],
                        inc=("c_act", 1, ("krowev", kidx)), tag=f"krowev{kidx}")
            em.op(e, lambda g, c=ch: g.copy(D["RBC"][c % 2][:, :],
                                            D["RB"][:, :]),
                  waits=[("c_pe", ("rb", ch))],
                  inc=("c_act", 1, ("rbev", ch)), tag=f"rbev{ch}")

    def p0b_dve():
        e = "dve"
        for ch in range(8):
            em.op(e, lambda g, c=ch: g.tensor_tensor(
                D["KTS"][:, 512 * c:512 * c + 512], D["KP"][c % 2][:, :],
                D["RBC"][c % 2][:, :], ALU.mult),
                waits=[("c_pe", ("kv", ch)), ("c_act", ("rbev", ch))],
                tag=f"kevK{ch}")
            wl2 = [("c_pe", ("vt", (ch - 2) * 4 + 3))] if ch >= 2 else []
            em.op(e, lambda g, c=ch: g.tensor_tensor(
                D["VTTB"][c % 2][:, :], D["VP"][c % 2][:, :],
                D["RBC"][c % 2][:, :], ALU.mult),
                waits=wl2, inc=("c_dve", 1, ("kvev", ch)), tag=f"kevV{ch}")
            if ch >= 1:
                for l in range(4):
                    vi = (ch - 1) * 4 + l
                    em.op(e, lambda g, k=vi: g.tensor_copy(
                        D["VSB"][:, ts(k, 128)], D["PVT"][:, :]),
                        waits=[("c_pe", ("vt", vi))],
                        inc=("c_dve", 1, ("vtev", vi)), tag=f"vtev{vi}")
        for vi in range(7 * 4, KT):
            em.op(e, lambda g, k=vi: g.tensor_copy(
                D["VSB"][:, ts(k, 128)], D["PVT"][:, :]),
                waits=[("c_pe", ("vt", vi))],
                inc=("c_dve", 1, ("vtev", vi)), tag=f"vtev{vi}")

    def wb_idx(hf, kind, i):
        base = hf * (NH + KT)
        return base + (i if kind == "q" else NH + i)

    def p1_gp2():
        e = "gp"
        for hf in range(2):
            for h in range(NH):
                idx = wb_idx(hf, "q", h)
                if h >= 2:
                    wl = [("c_pe", ("qmm", li(hf, h - 2)))]
                elif hf == 1:
                    wl = [("c_pe", ("womm", (0, KT - 2 + h)))]
                else:
                    wl = []
                em.op(e, lambda g, hh=h, b=idx % 2: g.dma_start(
                    out=D["WB"][b][:, :], in_=D["wq"][hh]),
                    waits=wl, inc=dinc("wb", idx), tag=f"wq{hf}_{h}")
            for ot in range(KT):
                idx = wb_idx(hf, "o", ot)
                if ot >= 2:
                    wl = [("c_pe", ("womm", (hf, ot - 2)))]
                else:
                    wl = [("c_pe", ("qmm", li(hf, NH - 2 + ot)))]
                em.op(e, lambda g, o=ot, b=idx % 2: g.dma_start(
                    out=D["WB"][b][:, :], in_=D["wo"][o]),
                    waits=wl, inc=dinc("wb", idx), tag=f"wo{hf}_{ot}")

    def p1_sp():
        e = "sp"
        for hf in range(2):
            for h in range(NH):
                L = li(hf, h)
                em.op(e, lambda g, hh=h, b=L % 2: g.dma_start(
                    out=D["BRELB"][b][:, :], in_=D["brel"][hh]),
                    waits=[("c_dve", ("emask", L - 2))],
                    inc=dinc("brel", L), tag=f"brel{hf}_{h}")
            for ot in range(KT):
                gidx = hf * KT + ot
                em.op(e, lambda g, f=hf, o=ot, b=gidx % 2: g.dma_start(
                    out=D["s_scr"][o][:, 512 * f:512 * f + 512],
                    in_=D["SSB"][b][:, :]),
                    waits=[("c_act", ("sev", (hf, ot)))],
                    inc=dinc("sst", gidx), tag=f"sst{hf}_{ot}")

    def p1_pe():
        e = "pe"
        for hf in range(2):
            for i in range(NH + 3):
                q = i
                if q < NH:
                    L = li(hf, q)
                    for kt in range(KT):
                        wl = []
                        if kt == 0:
                            wl = [dwait("wb", wb_idx(hf, "q", q)),
                                  ("c_act", ("qev", L - 1))]
                            if L == 0:
                                wl += [("c_dve", ("vtev", KT - 1)),
                                       ("c_dve", ("xnt", KT - 1))]
                        em.op(e, lambda g, f=hf, k=kt, b=wb_idx(hf, "q", q) % 2:
                              g.matmul(
                                  D["QPS"][:, :], D["WB"][b][:, ts(k, 128)],
                                  D["XNT"][:, k * 1024 + 512 * f:
                                           k * 1024 + 512 * f + 512],
                                  start=(k == 0), stop=(k == KT - 1)),
                              waits=wl,
                              inc=(("c_pe", 1, ("qmm", L))
                                   if kt == KT - 1 else None),
                              tag=f"q{hf}_{q}_{kt}")
                sH = i - 1
                if 0 <= sH < NH:
                    L = li(hf, sH)
                    for kt in range(KTN[hf]):
                        n = SP_N[hf][kt]
                        s0 = SP_S0[hf][kt]
                        wl = [("c_act", ("qev", L))] if kt == 0 else []
                        if kt >= 2:
                            wl.append(("c_act", ("exp", (L, kt - 2))))
                        else:
                            pk = ktn_of(L - 1) - 2 + (kt % 2)
                            wl.append(("c_act", ("exp", (L - 1, pk))))
                        em.op(e, lambda g, k=kt, nn=n, ss=s0, ll=L: g.matmul(
                            D["SC"][k % 2][:, 0:nn], D["KTS"][:, ts(k, 128)],
                            D["QH"][ll % 2][:, ss:512], start=True, stop=True),
                            waits=wl, inc=("c_pe", 1, ("sc", (L, kt))),
                            tag=f"sc{hf}_{sH}_{kt}")
                a = i - 2
                if 0 <= a < NH:
                    L = li(hf, a)
                    for kt in range(KTN[hf]):
                        n = SP_N[hf][kt]
                        s0 = SP_S0[hf][kt]
                        o = SP_OFF[hf][kt]
                        wl = []
                        if kt == 0:
                            wl = [("c_dve", ("emask", L)),
                                  ("c_dve", ("avev", L - 1)),
                                  ("c_act", ("dev", L - 1))]
                        em.op(e, lambda g, f=hf, k=kt, nn=n, ss=s0, oo=o, ll=L:
                              g.matmul(
                                  D["AVP"][:, ss:512], D["VSB"][:, ts(k, 128)],
                                  D["ES"][ll % 2][:, oo:oo + nn],
                                  start=(k == 0), stop=(k == KTN[f] - 1)),
                              waits=wl, tag=f"av{hf}_{a}_{kt}")
                        em.op(e, lambda g, f=hf, k=kt, nn=n, ss=s0, oo=o, ll=L:
                              g.matmul(
                                  D["DPS"][0:1, ss:512], D["ON16"][:, :],
                                  D["ES"][ll % 2][:, oo:oo + nn],
                                  start=(k == 0), stop=(k == KTN[f] - 1)),
                              inc=(("c_pe", 1, ("avd", L))
                                   if kt == KTN[hf] - 1 else None),
                              tag=f"dn{hf}_{a}_{kt}")
                o_ = i - 3
                if 0 <= o_ < NH:
                    L = li(hf, o_)
                    em.op(e, lambda g, ll=L: g.matmul(
                        D["DBP"][:, :], D["ONR"][:, :],
                        D["DIR"][ll % 2][0:1, :], start=True, stop=True),
                        waits=[("c_dve", ("recip", L)),
                               ("c_dve", ("div", L - 1))],
                        inc=("c_pe", 1, ("outer", L)), tag=f"ou{hf}_{o_}")
            for ot in range(KT):
                for kt in range(KT):
                    wl = []
                    if kt == 0:
                        wl = [dwait("wb", wb_idx(hf, "o", ot))]
                        if ot >= 2:
                            wl.append(("c_act", ("sev", (hf, ot - 2))))
                        elif hf == 1:
                            wl.append(("c_act", ("sev", (0, KT - 2 + ot))))
                        if ot == 0:
                            wl.append(("c_dve", ("div", li(hf, NH - 1))))
                    em.op(e, lambda g, o=ot, k=kt, b=wb_idx(hf, "o", ot) % 2:
                          g.matmul(
                              D["SWO"][o % 2][:, :], D["WB"][b][:, ts(k, 128)],
                              D["ATT"][:, ts(k, 512)],
                              start=(k == 0), stop=(k == KT - 1)),
                          waits=wl,
                          inc=(("c_pe", 1, ("womm", (hf, ot)))
                               if kt == KT - 1 else None),
                          tag=f"wo{hf}_{ot}_{kt}")

    def p1_act():
        e = "act"
        for hf in range(2):
            for i in range(NH + 3):
                q = i
                if q < NH:
                    L = li(hf, q)
                    wl = [("c_pe", ("qmm", L))]
                    if L >= 2:
                        wl.append(("c_pe", ("sc", (L - 2, ktn_of(L - 2) - 1))))
                    em.op(e, lambda g, ll=L: g.copy(
                        D["QH"][ll % 2][:, :], D["QPS"][:, :]),
                        waits=wl, inc=("c_act", 1, ("qev", L)),
                        tag=f"qev{hf}_{q}")
                sH = i - 1
                if 0 <= sH < NH:
                    L = li(hf, sH)
                    for kt in range(KTN[hf]):
                        o = SP_OFF[hf][kt]
                        lo = max(umin(kt), 4 * hf)
                        wl = [("c_pe", ("sc", (L, kt)))]
                        if kt == 0:
                            wl += [("c_pe", ("avd", L - 2)),
                                   dwait("brel", L)]
                        nch = 4 * hf + 4 - lo
                        for ui in range(nch):
                            u = lo + ui
                            pi = PAIR_IDX[(kt, u)]
                            em.op(e, lambda g, k=kt, oo=o, uu=ui, ll=L, p=pi:
                                  g.activation(
                                      D["ES"][ll % 2][:, oo + 128 * uu:
                                                      oo + 128 * uu + 128],
                                      D["SC"][k % 2][:, 128 * uu:
                                                     128 * uu + 128],
                                      AF.Exp,
                                      bias=D["BRELB"][ll % 2][:, p:p + 1],
                                      scale=SCALE),
                                  waits=(wl if ui == 0 else []),
                                  inc=(("c_act", 1, ("exp", (L, kt)))
                                       if ui == nch - 1 else None),
                                  tag=f"ex{hf}_{sH}_{kt}_{u}")
                d = i - 2
                if 0 <= d < NH:
                    L = li(hf, d)
                    em.op(e, lambda g, ll=L: g.copy(
                        D["DRO"][ll % 2][0:1, :], D["DPS"][0:1, :]),
                        waits=[("c_pe", ("avd", L)),
                               ("c_dve", ("recip", L - 2))],
                        inc=("c_act", 1, ("dev", L)), tag=f"dev{hf}_{d}")
            for ot in range(KT):
                gidx = hf * KT + ot
                em.op(e, lambda g, o=ot, b=gidx % 2: g.copy(
                    D["SSB"][b][:, :], D["SWO"][o % 2][:, :]),
                    waits=[("c_pe", ("womm", (hf, ot))),
                           dwait("sst", gidx - 2)],
                    inc=("c_act", 1, ("sev", (hf, ot))), tag=f"sev{hf}_{ot}")

    def p1_dve():
        e = "dve"
        for hf in range(2):
            for i in range(NH + 4):
                v = i - 4
                if 0 <= v < NH:
                    L = li(hf, v)
                    wl = [("c_pe", ("outer", L))]
                    if hf == 1 and v == 0:
                        wl.append(("c_pe", ("womm", (0, KT - 1))))
                    wl.append(("c_dve", ("avev", L)))
                    em.op(e, lambda g, ll=L, vv=v: g.tensor_tensor(
                        D["ATT"][:, ts(vv, 512)], D["AVR"][ll % 2][:, :],
                        D["DBP"][:, :], ALU.mult),
                        waits=wl, inc=("c_dve", 1, ("div", L)),
                        tag=f"dv{hf}_{v}")
                m = i - 1
                if 0 <= m < NH:
                    L = li(hf, m)
                    nmask = 0
                    for kt in range(KTN[hf]):
                        if umin(kt) >= 4 * hf:
                            o = SP_OFF[hf][kt]
                            nmask += 1
                            em.op(e, lambda g, ll=L, oo=o, m4=kt % 4:
                                  g.tensor_tensor(
                                      D["ES"][ll % 2][:, oo:oo + 128],
                                      D["ES"][ll % 2][:, oo:oo + 128],
                                      D["MSK"][:, ts(m4, 128)], ALU.mult),
                                  waits=[("c_act", ("exp", (L, kt)))],
                                  inc=(("c_dve", 1, ("emask", L))
                                       if nmask == 16 else None),
                                  tag=f"mk{hf}_{m}_{kt}")
                a = i - 2
                if 0 <= a < NH:
                    L = li(hf, a)
                    em.op(e, lambda g, ll=L: g.tensor_copy(
                        D["AVR"][ll % 2][:, :], D["AVP"][:, :]),
                        waits=[("c_pe", ("avd", L)), ("c_dve", ("div", L - 2))],
                        inc=("c_dve", 1, ("avev", L)), tag=f"ae{hf}_{a}")
                    em.op(e, lambda g, ll=L: g.reciprocal(
                        D["DIR"][ll % 2][0:1, :], D["DRO"][ll % 2][0:1, :]),
                        waits=[("c_act", ("dev", L)),
                               ("c_pe", ("outer", L - 2))],
                        inc=("c_dve", 1, ("recip", L)), tag=f"rc{hf}_{a}")

    def wli(sp, wi):
        return sp * 2 + wi

    def p2_gp():
        e = "gp"
        for sp in range(NSPE):
            for wi in range(2):
                wb = wli(sp, wi)
                svsrc = D["svg"] if wi == 0 else D["svu"]
                em.op(e, lambda g, sr=svsrc: g.dma_start(
                    out=D["SVB"][:, :], in_=sr),
                    waits=[("c_pe", ("cgenp", ((wb - 1) * KT + KT - 1) // 2))],
                    inc=dinc("sv2", wb), tag=f"sv{wb}")
                susrc = D["sug"] if wi == 0 else D["suu"]
                em.op(e, lambda g, s=sp, sr=susrc, b=wb % 2: g.dma_start(
                    out=D["SUB"][b][:, :],
                    in_=sr[:, 1024 * s:1024 * s + 1024]),
                    waits=[("c_pe", ("cgenp", ((wb - 2) * KT + KT - 1) // 2))],
                    inc=dinc("su", wb), tag=f"su{wb}")
                srcw = D["gw"] if wi == 0 else D["uw"]
                for lb4 in range(KT // 4):
                    lb = wb * (KT // 4) + lb4
                    wl = []
                    if lb >= 2:
                        wl = [("c_dve", ("weff4", lb - 2))]
                    em.op(e, lambda g, s=sp, k4=lb4, sr=srcw, b=lb % 3:
                          g.dma_start(
                              out=D["GWB"][b][:, :].rearrange(
                                  "p (k c) -> p k c", k=4),
                              in_=sr[s][4 * k4:4 * k4 + 4].rearrange(
                                  "k p c -> p k c")),
                          waits=wl, inc=(f"d_gwb{lb % 3}", 16, ("gwb", lb)),
                          tag=f"gw{lb}")

    def p2_mm(sp, wi, itl, kk, gi_weff):
        e = "pe"
        wl = []
        if kk == 0:
            if itl == 0:
                pass
            elif wi == 0 and P2MODE != 5:
                wl.append(("c_act", ("silu", (sp, itl - 1))))
            elif wi == 1:
                wl.append(("c_dve", ("dmul", (sp, itl - 1))))
            if itl == 0:
                key = (sp - 1, ITL - 1)
                if wi == 0 and P2MODE != 5:
                    wl.append(("c_act", ("silu", key)))
                elif wi == 1:
                    wl.append(("c_dve", ("dmul", key)))
        if gi_weff is not None:
            wl.append(("c_dve", ("weff4", gi_weff // 4)))
        elif kk == 0 and itl == 1:
            wl.append(("c_dve", ("weff4", (wli(sp, wi) * KT + KT - 1) // 4)))
        psum = D["APS"] if wi == 0 else D["BPS"]
        for half in range(2):
            em.op(e, lambda g, it=itl, k=kk, hh=half, p=psum: g.matmul(
                p[:, 512 * hh:512 * hh + 512],
                D["WEFF"][:, k * 1024 + 128 * it:k * 1024 + 128 * it + 128],
                D["XNT"][:, k * 1024 + 512 * hh:k * 1024 + 512 * hh + 512],
                start=(k == 0), stop=(k == KT - 1)),
                waits=(wl if half == 0 else []),
                inc=(("c_pe", 1, (("gmm" if wi == 0 else "umm"), (sp, itl)))
                     if (kk == KT - 1 and half == 1) else None),
                tag=f"mm{wi}_{sp}_{itl}_{kk}")

    def p2_mm_half(sp, wi, itl, kk, gi_weff, hh):
        e = "pe"
        wl = []
        if gi_weff is not None:
            wl.append(("c_dve", ("weff", gi_weff)))
        elif kk == 0:
            wl.append(("c_dve", ("weff", wli(sp, wi) * KT + KT - 1)))
        psum = D["APS"] if wi == 0 else D["BPS"]
        em.op(e, lambda g, it=itl, k=kk, p=psum: g.matmul(
            p[:, 512 * hh:512 * hh + 512],
            D["WEFF"][:, k * 1024 + 128 * it:k * 1024 + 128 * it + 128],
            D["XNT"][:, k * 1024 + 512 * hh:k * 1024 + 512 * hh + 512],
            start=(k == 0), stop=(k == KT - 1)),
            waits=wl,
            inc=(("c_pe", 1, (("gmm" if wi == 0 else "umm"), (sp, itl)))
                 if (kk == KT - 1 and hh == 1) else None),
            tag=f"mmh{wi}_{sp}_{itl}_{kk}_{hh}")

    def p2_pe():
        e = "pe"
        for sp in range(NSPE):
            for wi in range(2):
                wb = wli(sp, wi)
                sv = D["SVB"]
                NOIL = int(os.environ.get("KNOIL", "0"))
                for k in range(KT + 6 if not NOIL else KT):
                    if k < KT:
                        gi = wb * KT + k
                        wl = [("c_act", ("c16p", (gi - 2) // 2))]
                        if k == 0:
                            wl += [dwait("su", wb), dwait("sv2", wb)]
                        for half in range(2):
                            em.op(e, lambda g, kk=k, hh=half, svv=sv,
                                  b=wb % 2: g.matmul(
                                      D["CPS"][kk % 2][:, 512 * hh:
                                                       512 * hh + 512],
                                      svv[:, ts(kk, 128)],
                                      D["SUB"][b][:, 512 * hh:512 * hh + 512],
                                      start=True, stop=True),
                                  waits=(wl if half == 0 else []),
                                  inc=(("c_pe", 1, ("cgenp", gi // 2))
                                       if (half == 1 and gi % 2 == 1)
                                       else None),
                                  tag=f"cg{wb}_{k}")
                    if (not NOIL) and k >= 6 and P2MODE >= (2 if wi == 0 else 6):
                        p2_mm(sp, wi, 0, k - 6, wb * KT + (k - 6))
                if P2MODE >= (2 if wi == 0 else 6):
                    if NOIL:
                        if int(os.environ.get("KSEQMM", "0")):
                            for hh in range(2):
                                for kk in range(KT):
                                    p2_mm_half(sp, wi, 0, kk,
                                               wb * KT + kk if hh == 0 else None,
                                               hh)
                        else:
                            for kk in range(KT):
                                p2_mm(sp, wi, 0, kk, wb * KT + kk)
                    for itl in range(1, ITLE):
                        for kk in range(KT):
                            p2_mm(sp, wi, itl, kk, None)


    def p2_act():
        e = "act"
        for sp in range(NSPE):
            for wi in range(2):
                wb = wli(sp, wi)
                for kt in range(KT):
                    gi = wb * KT + kt
                    wl = [("c_pe", ("cgenp", gi // 2))]
                    if gi >= 6 and os.environ.get("KC16WAR", "1") == "1":
                        wl.append(("c_dve", ("weff4", (gi - 6) // 4)))
                    em.op(e, lambda g, k=kt, b=gi % 6: g.copy(
                        D["C16"][b][:, :], D["CPS"][k % 2][:, :]),
                        waits=wl,
                        inc=(("c_act", 1, ("c16p", gi // 2))
                             if gi % 2 == 1 else None), tag=f"c16_{gi}")
                if wi == 0 and (P2MODE >= 3 or P2MODE == 7):
                    sfunc = os.environ.get("KSILU", "Silu")
                    def _sil(g, it, ff=sfunc):
                        if ff == "helpercopy":
                            return g.copy(D["SILA"][:, ts(it, 1024)],
                                          D["APS"][:, :])
                        if ff == "helpercps":
                            return g.copy(D["SILA"][:, ts(it, 1024)],
                                          D["CPS"][0][:, :])
                        if ff == "c16out":
                            return g.activation(D["C16"][0][:, :],
                                                D["APS"][:, :], AF.Silu)
                        if ff == "half0":
                            return g.copy(D["SILA"][:, 0:512],
                                          D["APS"][:, 0:512])
                        if ff == "half1":
                            return g.copy(D["SILA"][:, 512:1024],
                                          D["APS"][:, 512:1024])
                        if ff == "bpsread":
                            return g.copy(D["SILA"][:, 0:512],
                                          D["BPS"][:, 0:512])
                        if ff == "twohalf":
                            g.copy(D["SILA"][:, 0:512], D["APS"][:, 0:512])
                            return g.copy(D["SILA"][:, 512:1024],
                                          D["APS"][:, 512:1024])
                        return g.activation(D["SILA"][:, ts(it, 1024)],
                                            D["APS"][:, :], getattr(AF, ff))
                    for itl in range(ITLE):
                        em.op(e, lambda g, s=sp, it=itl: _sil(g, it),
                            waits=[("c_pe", ("gmm", (sp, itl)))],
                            inc=("c_act", 1, ("silu", (sp, itl))),
                            tag=f"sil{sp}_{itl}")

    def p2_dve():
        e = "dve"
        for sp in range(NSPE):
            for wi in range(2):
                wb = wli(sp, wi)
                for kt in range(KT):
                    gi = wb * KT + kt
                    lb = gi // 4
                    em.op(e, lambda g, k=kt, b=lb % 3, b2=gi % 6, q=gi % 4:
                          g.tensor_tensor(
                              D["WEFF"][:, ts(k, 1024)],
                              D["GWB"][b][:, 1024 * q:1024 * q + 1024],
                              D["C16"][b2][:, :], ALU.mult),
                          waits=[("c_act", ("c16p", gi // 2)),
                                 (f"d_gwb{lb % 3}", ("gwb", lb)),
                                 ("c_pe", (("gmm" if (wb - 1) % 2 == 0
                                            else "umm"),
                                           ((wb - 1) // 2, ITL - 1)))],
                          inc=(("c_dve", 1, ("weff4", gi // 4))
                               if gi % 4 == 3 else None), tag=f"wf{gi}")
                if wi == 1 and P2MODE >= 6:
                    for itl in range(ITL):
                        it_g = sp * ITL + itl
                        em.op(e, lambda g, s=sp, it=itl, b=it_g % 2:
                              g.tensor_tensor(
                                  D["DST"][b][:, :], D["SILA"][:, ts(it, 1024)],
                                  D["BPS"][:, :], ALU.mult),
                              waits=[("c_pe", ("umm", (sp, itl))),
                                     ("c_act", ("silu", (sp, itl))),
                                     dwait("dstore", it_g - 2)],
                              inc=("c_dve", 1, ("dmul", (sp, itl))),
                              tag=f"dm{sp}_{itl}")

    def p2_sp():
        e = "sp"
        if P2MODE < 6:
            return
        for sp in range(NSPE):
            for itl in range(ITL):
                it_g = sp * ITL + itl
                em.op(e, lambda g, ig=it_g, b=it_g % 2: g.dma_start(
                    out=D["d_scr"][ig // 4, :, ig % 4, :],
                    in_=D["DST"][b][:, :]),
                    waits=[("c_dve", ("dmul", (sp, itl)))],
                    inc=dinc("dstore", it_g), tag=f"dsp{it_g}")

    def p3_gp():
        e = "gp"
        em.op(e, lambda g: g.dma_start(out=D["SUDB"][:, :], in_=D["sud"]),
              inc=("d_sv3", 16, None), tag="sudl")
        def svd_load(oq, cc):
            lc = oq * 8 + cc
            if cc >= 2:
                wl = [("c_pe", ("cgdp",
                                (oq * NIT + 14 * (cc - 2) + 13) // 2))]
            elif oq >= 1:
                wl = [("c_pe", ("cgdp", ((oq - 1) * NIT + NIT - 1) // 2))]
            else:
                wl = []
            em.op(e, lambda g, c=cc, b=lc % 2: g.dma_start(
                out=D["SVDB"][b][:, :],
                in_=D["svd"][:, 1792 * c:1792 * c + 1792]),
                waits=wl, inc=dinc("svd", lc), tag=f"svd{lc}")

        for oq in range(NOQE):
            svd_load(oq, 0)
            svd_load(oq, 1)
            for lb4 in range(NIT // 4):
                lb = oq * (NIT // 4) + lb4
                wl = []
                if lb >= 2:
                    wl = [("c_dve", ("weffd4", lb - 2))]
                em.op(e, lambda g, o=oq, k4=lb4, b=lb % 3: g.dma_start(
                    out=D["DWB"][b][:, :].rearrange(
                        "p (k c) -> p k c", k=4),
                    in_=D["dw"][o][4 * k4:4 * k4 + 4].rearrange(
                        "k p c -> p k c")),
                    waits=wl, inc=(f"d_dwb{lb % 3}", 16, ("dwb", lb)),
                    tag=f"dw{lb}")
                if lb4 >= 4 and lb4 % 4 == 0 and lb4 // 4 + 1 <= 7:
                    svd_load(oq, lb4 // 4 + 1)
            for t in range(4):
                og = oq * 4 + t
                em.op(e, lambda g, gg=og, b=og % 2: g.dma_start(
                    out=D["outT"][gg], in_=D["OUTB"][b][:, :]),
                    waits=[("c_dve", ("comb", og))],
                    inc=dinc("ost", og), tag=f"ost{og}")
        em.op(e, None, waits=[dwait("ost", NOQ * 4 - 2),
                              dwait("ost", NOQ * 4 - 1)], tag="finwait")

    def p3_sp():
        e = "sp"
        for oq in range(NOQE):
            for lb4 in range(NIT // 4):
                lb = oq * (NIT // 4) + lb4
                if lb4 >= 2:
                    wl = [("c_pe", ("dnm", (oq, 4 * (lb4 - 2) + 3)))]
                elif oq >= 1:
                    wl = [("c_pe", ("dnm", (oq - 1, NIT - 8 + 4 * lb4 + 3)))]
                else:
                    wl = []
                if lb == 0:
                    wl += [dwait("dstore", NIT - 2), dwait("dstore", NIT - 1)]
                em.op(e, lambda g, k4=lb4, b=lb % 3: g.dma_start(
                    out=D["DTB"][b][:, :],
                    in_=D["d_scr"][k4].rearrange("p t c -> p (t c)")),
                    waits=wl, inc=(f"d_dtb{lb % 3}", 16, ("dtb", lb)),
                    tag=f"dt{lb}")
            for t in range(4):
                og = oq * 4 + t
                wl = [("c_dve", ("comb", og - 2))]
                if og == 0:
                    wl += [dwait("sst", 2 * KT - 2), dwait("sst", 2 * KT - 1)]
                em.op(e, lambda g, o=og, b=og % 2: g.dma_start(
                    out=D["SLB"][b][:, :], in_=D["s_scr"][o]),
                    waits=wl, inc=dinc("sl", og), tag=f"sl{og}")
                em.op(e, lambda g, o=og, b=og % 2: g.dma_start(
                    out=D["XQB"][b][:, :], in_=D["xqT"][o]),
                    inc=dinc("xq2", og), tag=f"xq2{og}")

    def p3_pe():
        e = "pe"
        for oq in range(NOQE):
            for ik in range(NIT):
                gi = oq * NIT + ik
                wl = [("c_act", ("c16dp", (gi - 2) // 2)),
                      dwait("svd", oq * 8 + ik // 14)]
                if ik == 0:
                    wl.append(("c_act", ("gsil", (oq - 1, 0))))
                    wl.append(("d_sv3", 16))
                em.op(e, lambda g, o=oq, k=ik, b=gi % 2: g.matmul(
                    D["OUT"][0][:, 512 * b:512 * b + 512],
                    D["SVDB"][(oq * 8 + k // 14) % 2][:, ts(k % 14, 128)],
                    D["SUDB"][:, 512 * o:512 * o + 512],
                    start=True, stop=True),
                    waits=wl, inc=(("c_pe", 1, ("cgdp", gi // 2))
                                   if gi % 2 == 1 else None),
                    tag=f"cgd{gi}")
            for ik in range(NIT):
                gi = oq * NIT + ik
                wl = [("c_dve", ("weffd4", gi // 4)),
                      (f"d_dtb{(gi // 4) % 3}", ("dtb", gi // 4))]
                if ik == 0:
                    wl.append(("c_act", ("c16dp", (oq * NIT + NIT - 1) // 2)))
                    for t in range(4):
                        wl.append(("c_act", ("gsil", (oq - 1, t))))
                for t in range(4):
                    for half in range(2):
                        em.op(e, lambda g, k=ik, tt=t, hh=half,
                              b=(gi // 4) % 3, q=gi % 4: g.matmul(
                                  D["OUT"][tt][:, 512 * hh:512 * hh + 512],
                                  D["WEFD"][:, k * 512 + 128 * tt:
                                            k * 512 + 128 * tt + 128],
                                  D["DTB"][b][:, 1024 * q + 512 * hh:
                                              1024 * q + 512 * hh + 512],
                                  start=(k == 0), stop=(k == NIT - 1)),
                              waits=(wl if (t == 0 and half == 0) else []),
                              inc=(("c_pe", 1, ("dnm", (oq, ik)))
                                   if (t == 3 and half == 1) else None),
                              tag=f"dnm{oq}_{ik}_{t}")

    def p3_act():
        e = "act"
        for oq in range(NOQE):
            for ik in range(NIT):
                gi = oq * NIT + ik
                wl = [("c_pe", ("cgdp", gi // 2))]
                if gi >= 6:
                    wl.append(("c_dve", ("weffd4", (gi - 6) // 4)))
                em.op(e, lambda g, b=gi % 6, b2=gi % 2: g.copy(
                    D["C16D"][b][:, :],
                    D["OUT"][0][:, 512 * b2:512 * b2 + 512]),
                    waits=wl,
                    inc=(("c_act", 1, ("c16dp", gi // 2))
                         if gi % 2 == 1 else None), tag=f"c16d{gi}")
            for t in range(4):
                og = oq * 4 + t
                em.op(e, lambda g, o=oq, tt=t, b=og % 2: g.activation(
                    D["GB"][b][:, :], D["OUT"][tt][:, :], AF.Silu),
                    waits=[("c_pe", ("dnm", (oq, NIT - 1))),
                           ("c_dve", ("comb", og - 2))],
                    inc=("c_act", 1, ("gsil", (oq, t))), tag=f"gs{og}")

    def p3_dve():
        e = "dve"
        for oq in range(NOQE):
            for ik in range(NIT):
                gi = oq * NIT + ik
                lb = gi // 4
                em.op(e, lambda g, k=ik, b=lb % 3, b2=gi % 6, q=gi % 4:
                      g.tensor_tensor(
                          D["WEFD"][:, k * 512:(k + 1) * 512],
                          D["DWB"][b][:, 512 * q:512 * q + 512],
                          D["C16D"][b2][:, :], ALU.mult),
                      waits=[("c_act", ("c16dp", gi // 2)),
                             (f"d_dwb{lb % 3}", ("dwb", lb)),
                             ("c_pe", ("dnm", (oq - 1, NIT - 1)))],
                      inc=(("c_dve", 1, ("weffd4", gi // 4))
                           if gi % 4 == 3 else None), tag=f"wfd{gi}")
            for t in range(4):
                og = oq * 4 + t
                em.op(e, lambda g, o=oq, tt=t, b=og % 2: g.tensor_tensor(
                    D["SGB"][:, :], D["GB"][b][:, :], D["SLB"][b][:, :],
                    ALU.mult),
                    waits=[("c_act", ("gsil", (oq, t))), dwait("sl", og),
                           dwait("xq2", og), dwait("ost", og - 2)],
                    inc=("c_dve", 1, ("sg", og)), tag=f"sg{og}")
                em.op(e, lambda g, b=og % 2: g.tensor_tensor(
                    D["OUTB"][b][:, :], D["SGB"][:, :], D["XQB"][b][:, :],
                    ALU.add),
                    waits=[("c_dve", ("sg", og))],
                    inc=("c_dve", 1, ("comb", og)), tag=f"cb{og}")

    phases = [
        (1, {"sp": p0a_sp, "gp": p0a_gp, "pe": p0a_pe, "act": p0a_act,
             "dve": p0a_dve}),
        (2, {"gp": p0b_gp, "pe": p0b_pe, "act": p0b_act, "dve": p0b_dve}),
        (3, {"gp": p1_gp2, "sp": p1_sp, "pe": p1_pe, "act": p1_act,
             "dve": p1_dve}),
        (4, {"gp": p2_gp, "sp": p2_sp, "pe": p2_pe, "act": p2_act,
             "dve": p2_dve}),
        (5, {"gp": p3_gp, "sp": p3_sp, "pe": p3_pe, "act": p3_act,
             "dve": p3_dve}),
    ]
    phlim = int(os.environ.get("KPHASES", "5"))
    phskip = int(os.environ.get("KPHSKIP", "0"))
    if os.environ.get("KSIMINIT", "0") == "1":
        for tname in ("XNT", "KTS", "VSB", "MSK"):
            em.op("gp", lambda g, t=tname: g.memset(D[t][:, :], 0.25),
                  inc=("d_init", 1, None), tag=f"init{tname}")
        for e in ENGS:
            em.op(e, None, waits=[("d_init", 4)], tag="initw")
    for ph, emap in phases:
        if ph > phlim:
            break
        if ph <= phskip:
            BAR0[0] = 5 * ph
            continue
        for e in ENGS:
            bar_wait(e, ph - 1)
            if e in emap:
                emap[e]()
            bar_inc(e, ph)


def build_program():
    nc = bass.Bass("TRN2", target_bir_lowering=False,
                   detect_race_conditions=(os.environ.get("KRACE", "0") == "1"))
    D = {}

    def dram_in(name, shape):
        D[name] = nc.dram_tensor(name, list(shape), F32,
                                 kind="ExternalInput").ap()

    dram_in("xk", (32, 128, 4096))
    dram_in("xq", (8, 128, 4096))
    dram_in("xkT", (32, 128, 4096))
    dram_in("xqT", (32, 128, 1024))
    dram_in("wq", (32, 128, 4096))
    dram_in("wkv", (128, 8192))
    dram_in("wo", (32, 128, 4096))
    dram_in("gw", (14, 32, 128, 1024))
    dram_in("uw", (14, 32, 128, 1024))
    dram_in("dw", (8, 112, 128, 512))
    dram_in("sug", (128, 14336))
    dram_in("svg", (128, 4096))
    dram_in("suu", (128, 14336))
    dram_in("svu", (128, 4096))
    dram_in("sud", (128, 4096))
    dram_in("svd", (128, 14336))
    dram_in("masks", (128, 512))
    dram_in("brel", (32, 128, NPAIR))
    dram_in("ident", (128, 128))
    dram_in("ones_col", (128, 1))
    dram_in("ones_row", (1, 128))
    dram_in("epsc", (128, 1))
    D["d_scr"] = nc.dram_tensor("d_scr", [28, 128, 4, 1024], BF16,
                                kind="Internal").ap()
    D["s_scr"] = nc.dram_tensor("s_scr", [32, 128, 1024], F32,
                                kind="Internal").ap()
    D["outT"] = nc.dram_tensor("outT", [32, 128, 1024], F32,
                               kind="ExternalOutput").ap()

    def sb(st, name, shape, dt):
        D[name] = st.enter_context(nc.sbuf_tensor(name, list(shape), dt))

    def sbl(st, name, n, shape, dt):
        D[name] = [st.enter_context(
            nc.sbuf_tensor(f"{name}{i}", list(shape), dt)) for i in range(n)]

    def ps(st, name, shape, dt=F32):
        D[name] = st.enter_context(nc.psum_tensor(name, list(shape), dt))

    def psl(st, name, n, shape, dt=F32):
        D[name] = [st.enter_context(
            nc.psum_tensor(f"{name}{i}", list(shape), dt)) for i in range(n)]

    top = contextlib.ExitStack()
    sb(top, "ID32", (128, 128), F32)
    sb(top, "ID16", (128, 128), BF16)
    sb(top, "ONR", (1, 128), F32)
    sb(top, "ON16", (128, 1), BF16)
    sb(top, "EPSC", (128, 1), F32)
    sb(top, "CKR", (1, 128), F32)
    sb(top, "CVR", (1, 128), F32)
    sb(top, "CSB", (128, 2), F32)

    s_xnt = contextlib.ExitStack()
    sb(s_xnt, "XNT", (128, KT * 1024), BF16)

    s_p01 = contextlib.ExitStack()
    sb(s_p01, "KTS", (128, 4096), BF16)
    sb(s_p01, "VSB", (128, 4096), BF16)
    sbl(s_p01, "WB", 2, (128, 4096), BF16)
    sb(s_p01, "MSK", (128, 512), BF16)

    s_p0 = contextlib.ExitStack()
    sb(s_p0, "WKV", (128, 8192), BF16)
    sb(s_p0, "MNEG", (128, 40), F32)
    sb(s_p0, "RSTD", (128, 40), F32)
    sbl(s_p0, "MRB", 2, (1, 512), F32)
    sbl(s_p0, "RRB", 2, (1, 512), F32)
    sb(s_p0, "MRO_OWN", (1, 1024), F32)
    sb(s_p0, "RR_OWN", (1, 1024), F32)
    sb(s_p0, "MR_OWN", (1, 1024), F32)
    s_p0a = contextlib.ExitStack()
    sb(s_p0a, "XKB0", (128, 4096), F32)
    sb(s_p0a, "XKB1", (128, 4096), F32)
    D["XKB"] = [D["XKB0"], D["XKB1"]]
    sbl(s_p0a, "XQTB", 2, (128, 1024), F32)
    sb(s_p0a, "SCR", (128, 4096), F32)
    sb(s_p0a, "SUMS", (128, 40), F32)
    sb(s_p0a, "SUMSQ", (128, 40), F32)
    sb(s_p0a, "EX2", (128, 40), F32)
    sb(s_p0a, "MSQ", (128, 40), F32)
    sb(s_p0a, "LNV", (128, 40), F32)
    sb(s_p0a, "RBCO", (128, 1024), F32)
    sb(s_p0a, "MRBCO", (128, 1024), F32)
    s_p0a.close()
    s_p0b = contextlib.ExitStack()
    sbl(s_p0b, "SLAB", 2, (128, KT * 512), BF16)
    sbl(s_p0b, "RBC", 2, (128, 512), F32)
    sbl(s_p0b, "VTTB", 2, (128, 512), BF16)
    s_p0b.close()
    psl(s_p0, "KP", 2, (128, 512))
    psl(s_p0, "VP", 2, (128, 512))
    ps(s_p0, "CS", (128, 2))
    ps(s_p0, "TR", (1, 128))
    ps(s_p0, "RB", (128, 512))
    ps(s_p0, "PVT", (128, 128), BF16)
    s_p0.close()

    s_p1 = contextlib.ExitStack()
    sbl(s_p1, "ES", 2, (128, ECOLS), BF16)
    sb(s_p1, "ATT", (128, NH * 512), BF16)
    sbl(s_p1, "QH", 2, (128, 512), BF16)
    sbl(s_p1, "BRELB", 2, (128, NPAIR), F32)
    sbl(s_p1, "AVR", 2, (128, 512), F32)
    sbl(s_p1, "DRO", 2, (1, 512), F32)
    sbl(s_p1, "DIR", 2, (1, 512), F32)
    sbl(s_p1, "SSB", 2, (128, 512), F32)
    ps(s_p1, "QPS", (128, 512))
    psl(s_p1, "SC", 2, (128, 512))
    ps(s_p1, "AVP", (128, 512))
    ps(s_p1, "DPS", (1, 512))
    ps(s_p1, "DBP", (128, 512))
    psl(s_p1, "SWO", 2, (128, 512))
    s_p1.close()
    s_p01.close()

    s_p2 = contextlib.ExitStack()
    sb(s_p2, "WEFF", (128, KT * 1024), BF16)
    sb(s_p2, "SILA", (128, ITL * 1024), BF16)
    sbl(s_p2, "GWB", 3, (128, 4096), BF16)
    sbl(s_p2, "C16", 6, (128, 1024), BF16)
    sbl(s_p2, "DST", 2, (128, 1024), BF16)
    sb(s_p2, "SVB", (128, 4096), BF16)
    sbl(s_p2, "SUB", 2, (128, 1024), BF16)
    psl(s_p2, "CPS", 2, (128, 1024))
    ps(s_p2, "BPS", (128, 1024))
    ps(s_p2, "APS", (128, 1024))
    s_p2.close()
    s_xnt.close()

    s_p3 = contextlib.ExitStack()
    sb(s_p3, "WEFD", (128, NIT * 512), BF16)
    sb(s_p3, "SUDB", (128, 4096), BF16)
    sbl(s_p3, "SVDB", 2, (128, 1792), BF16)
    sbl(s_p3, "DWB", 3, (128, 2048), BF16)
    sbl(s_p3, "C16D", 6, (128, 512), BF16)
    sbl(s_p3, "DTB", 3, (128, 4096), BF16)
    sbl(s_p3, "SLB", 2, (128, 1024), F32)
    sbl(s_p3, "XQB", 2, (128, 1024), F32)
    sbl(s_p3, "GB", 2, (128, 1024), F32)
    sb(s_p3, "SGB", (128, 1024), F32)
    sbl(s_p3, "OUTB", 2, (128, 1024), F32)
    psl(s_p3, "OUT", 4, (128, 1024))
    s_p3.close()
    top.close()

    em = Em()
    build_phases(em, D)
    em.check()

    sems = contextlib.ExitStack()
    for s in sorted(em.count.keys()):
        em.semh[s] = sems.enter_context(nc.semaphore(s))
    em.emit = True
    with nc.Block() as block:
        @block.sync
        def _(eng):
            em.eng["sp"] = eng
            em.only = "sp"
            build_phases(em, D)

        @block.gpsimd
        def _(eng):
            em.eng["gp"] = eng
            em.only = "gp"
            build_phases(em, D)

        @block.tensor
        def _(eng):
            em.eng["pe"] = eng
            em.only = "pe"
            build_phases(em, D)

        @block.scalar
        def _(eng):
            em.eng["act"] = eng
            em.only = "act"
            build_phases(em, D)

        @block.vector
        def _(eng):
            em.eng["dve"] = eng
            em.only = "dve"
            build_phases(em, D)
    sems.close()
    return nc


_slopes = (2.0 ** (-2.0 ** (-(np.log2(NH) - 3)))) ** (np.arange(NH) + 1)
_slopes = _slopes.astype(np.float64)


def _pad128(a):
    out = np.zeros((128, a.shape[1]), dtype=np.float32)
    out[:a.shape[0]] = a
    return np.ascontiguousarray(out)


def _host_prep(inp):
    f32 = np.float32
    x = np.ascontiguousarray(inp["L0_attn_hidden_states"], dtype=f32)
    shared = {
        "wq": np.ascontiguousarray(
            np.asarray(inp["L0_attn_Wq"], f32).reshape(32, 128, 32, 128)
            .transpose(2, 1, 0, 3).reshape(32, 128, 4096)),
        "wkv": np.ascontiguousarray(
            np.asarray(inp["L0_attn_Wkv"], f32).reshape(32, 128, 256)
            .transpose(1, 0, 2).reshape(128, 8192)),
        "wo": np.ascontiguousarray(
            np.asarray(inp["L0_attn_Wo"], f32).reshape(32, 128, 32, 128)
            .transpose(2, 1, 0, 3).reshape(32, 128, 4096)),
        "gw": np.ascontiguousarray(
            np.asarray(inp["L0_ffn_gate_w"], f32).reshape(14, 1024, 32, 128)
            .transpose(0, 2, 3, 1)),
        "uw": np.ascontiguousarray(
            np.asarray(inp["L0_ffn_up_w"], f32).reshape(14, 1024, 32, 128)
            .transpose(0, 2, 3, 1)),
        "dw": np.ascontiguousarray(
            np.asarray(inp["L0_ffn_down_w"], f32).reshape(8, 512, 112, 128)
            .transpose(0, 2, 3, 1)),
        "sug": _pad128(np.asarray(inp["L0_ffn_gate_su"], f32).T),
        "svg": _pad128(np.asarray(inp["L0_ffn_gate_sv"], f32)),
        "suu": _pad128(np.asarray(inp["L0_ffn_up_su"], f32).T),
        "svu": _pad128(np.asarray(inp["L0_ffn_up_sv"], f32)),
        "sud": _pad128(np.asarray(inp["L0_ffn_down_su"], f32).T),
        "svd": _pad128(np.asarray(inp["L0_ffn_down_sv"], f32)),
        "ident": np.eye(128, dtype=f32),
        "ones_col": np.ones((128, 1), dtype=f32),
        "ones_row": np.ones((1, 128), dtype=f32),
        "epsc": np.full((128, 1), EPS, dtype=f32),
    }
    in_maps = []
    batch_cache = {}
    kl = np.arange(128)[:, None]
    ql = np.arange(128)[None, :]
    karr = np.arange(128, dtype=np.float64)
    for c in range(8):
        b, j = c // 4, c % 4
        if b not in batch_cache:
            xb = x[b]
            xbT = np.ascontiguousarray(xb.T)
            batch_cache[b] = (
                xb, xbT,
                np.ascontiguousarray(xb.reshape(32, 128, 4096)),
                np.ascontiguousarray(xbT.reshape(32, 128, 4096)))
        xb, xbT, xk_t, xkT_t = batch_cache[b]
        tiles = [4 * u + j for u in range(8)]
        xq = np.ascontiguousarray(
            np.stack([xb[128 * T:128 * T + 128] for T in tiles]))
        xqT = np.ascontiguousarray(
            np.concatenate([xbT[:, 128 * T:128 * T + 128] for T in tiles],
                           axis=1).reshape(32, 128, 1024))
        masks = np.zeros((128, 512), dtype=f32)
        for m in range(4):
            masks[:, 128 * m:128 * m + 128] = (
                (128 * m + kl) <= (128 * j + ql)).astype(f32)
        brel = np.zeros((32, 128, NPAIR), dtype=f32)
        for kt in range(KT):
            for u in range(umin(kt), 8):
                pi = PAIR_IDX[(kt, u)]
                K0 = 128.0 * (4 * u + j) + 64.0
                rel = (128.0 * kt + karr) - K0
                if u == umin(kt) and (kt % 4) > j:
                    brel[:, :, pi] = -30000.0
                else:
                    brel[:, :, pi] = (_slopes[:, None] *
                                      rel[None, :]).astype(f32)
        im = dict(shared)
        im.update({"xk": xk_t, "xq": xq, "xkT": xkT_t, "xqT": xqT,
                   "masks": masks, "brel": np.ascontiguousarray(brel)})
        in_maps.append(im)
    return in_maps


def kernel(**inputs):
    in_maps = _host_prep(inputs)
    nc = build_program()
    res = run_bass_kernel_spmd(nc, in_maps, core_ids=list(range(8)))
    out = np.empty((B, S, H), dtype=np.float32)
    for c in range(8):
        b, j = c // 4, c % 4
        oT = np.asarray(res.results[c]["outT"], dtype=np.float32)
        O = oT.transpose(2, 0, 1).reshape(1024, 4096)
        for u in range(8):
            T = 4 * u + j
            out[b, 128 * T:128 * T + 128, :] = O[128 * u:128 * u + 128]
    return out


# version 03 (final)
# speedup vs baseline: 1.0295x; 1.0295x over previous
import contextlib
import os
import numpy as np
import concourse.bass as bass
import concourse.mybir as mybir
from concourse.bass_utils import run_bass_kernel_spmd

F32 = mybir.dt.float32
BF16 = mybir.dt.bfloat16
AF = mybir.ActivationFunctionType
ALU = mybir.AluOpType
AX = mybir.AxisListType
ts = bass.ts

H = 4096
NH = 32
B = 2
S = 4096
KT = 32
NSP = 14
ITL = 8
NIT = 112
NOQ = 8
SCALE = float(1.0 / np.sqrt(128.0))
EPS = 1e-5
NT = 40


def umin(kt):
    return kt // 4


KTN = [16, 32]
SP_N = [[], []]
SP_S0 = [[], []]
SP_OFF = [[], []]
for _hf in range(2):
    _off = 0
    for _kt in range(KTN[_hf]):
        _lo = max(umin(_kt), 4 * _hf)
        _n = (4 * _hf + 4 - _lo) * 128
        SP_N[_hf].append(_n)
        SP_S0[_hf].append((_lo - 4 * _hf) * 128)
        SP_OFF[_hf].append(_off)
        _off += _n
ECOLS = max(sum(SP_N[0]), sum(SP_N[1]))

PAIR_IDX = {}
_pi = 0
for _kt in range(KT):
    for _u in range(umin(_kt), 8):
        PAIR_IDX[(_kt, _u)] = _pi
        _pi += 1
NPAIR = _pi

ENGS = ("sp", "gp", "pe", "act", "dve")


class Em:

    def __init__(self):
        self.emit = False
        self.only = None
        self.count = {}
        self.marks = {}
        self.stream = {e: [] for e in ENGS}
        self.eng = {e: None for e in ENGS}
        self.semh = {}
        self.wfloor = {}

    def mval(self, sem, key):
        if isinstance(key, int):
            return key
        return self.marks.get((sem, key), 0)

    def op(self, e, build, waits=(), inc=None, tag=""):
        if not self.emit:
            if inc is not None:
                sem, k, mk = inc
                self.count[sem] = self.count.get(sem, 0) + k
                if mk is not None:
                    self.marks[(sem, mk)] = self.count[sem]
            self.stream[e].append((waits, inc, tag))
            return
        if e != self.only:
            return
        eng = self.eng[e]
        floor = self.wfloor.setdefault(e, {})
        for sem, key in waits:
            val = self.mval(sem, key)
            if val > 0 and floor.get(sem, 0) < val:
                eng.wait_ge(self.semh[sem], val)
                floor[sem] = val
        instr = build(eng) if build is not None else None
        if inc is not None:
            sem, k, mk = inc
            instr.then_inc(self.semh[sem], k)

    def check(self):
        pc = {e: 0 for e in ENGS}
        cnt = {}
        progress = True
        while progress:
            progress = False
            for e in ENGS:
                while pc[e] < len(self.stream[e]):
                    waits, inc, tag = self.stream[e][pc[e]]
                    if any(cnt.get(s, 0) < self.mval(s, k) for s, k in waits):
                        break
                    if inc is not None:
                        cnt[inc[0]] = cnt.get(inc[0], 0) + inc[1]
                    pc[e] += 1
                    progress = True
        stuck = {}
        for e in ENGS:
            if pc[e] < len(self.stream[e]):
                w, _, tag = self.stream[e][pc[e]]
                stuck[e] = (pc[e], tag,
                            [(s, self.mval(s, k), cnt.get(s, 0)) for s, k in w])
        if stuck:
            raise RuntimeError(f"schedule deadlock: {stuck}")


def build_phases(em, D):
    ITLE = int(os.environ.get("KITL", str(ITL)))
    NSPE = int(os.environ.get("KNSP", str(NSP)))
    NOQE = int(os.environ.get("KNOQ", str(NOQ)))
    P2MODE = int(os.environ.get("KP2M", "9"))

    def li(hf, h):
        return hf * NH + h

    def ktn_of(L):
        return KTN[L // NH] if L >= 0 else KTN[0]

    def dinc(stream, idx):
        return (f"d_{stream}{idx % 2}", 16, (stream, idx))

    def dwait(stream, idx):
        return (f"d_{stream}{idx % 2}", (stream, idx))

    def dinc6(stream, idx):
        return (f"d_{stream}{idx % 6}", 16, (stream, idx))

    def dwait6(stream, idx):
        return (f"d_{stream}{idx % 6}", (stream, idx))

    BAR0 = [0]

    def bar_wait(e, ph):
        em.op(e, None, waits=[("bar", 5 * ph - BAR0[0])], tag=f"barw{ph}")

    def bar_inc(e, ph):
        em.op(e, lambda g: g.nop(), inc=("bar", 1, None), tag=f"bari{ph}")

    def p0a_sp():
        e = "sp"
        em.op(e, lambda g: g.dma_start(out=D["ID32"][:, :], in_=D["ident"]),
              inc=("d_msc", 16, None))
        em.op(e, lambda g: g.dma_start(out=D["ONR"][:, :], in_=D["ones_row"]),
              inc=("d_msc", 16, None))
        em.op(e, lambda g: g.dma_start(out=D["EPSC"][:, :], in_=D["epsc"]),
              inc=("d_msc", 16, None))
        for tt in range(NT):
            src = D["xk"][tt] if tt < 32 else D["xq"][tt - 32]
            em.op(e, lambda g, s=src, b=tt % 2: g.dma_start(
                out=D["XKB"][b][:, :], in_=s),
                waits=[("c_dve", ("dst", tt - 2)), ("c_act", ("ast", tt - 2))],
                inc=dinc("xk", tt), tag=f"xk{tt}")
        for ht in range(KT):
            em.op(e, lambda g, h=ht: g.dma_start(
                out=D["XQTB"][h % 2][:, :], in_=D["xqT"][h]),
                waits=[("c_dve", ("xnt", ht - 2))],
                inc=dinc("xqt", ht), tag=f"xqt{ht}")

    def p0a_gp():
        e = "gp"
        casts = [("WKV", "wkv"), ("ID16", "ident"), ("ON16", "ones_col"),
                 ("MSK", "masks")]
        for dst, src in casts:
            em.op(e, lambda g, d=dst, s=src: g.dma_start(out=D[d][:, :],
                                                         in_=D[s]),
                  inc=("d_cst", 16, None), tag=f"cst{dst}")

    CST_ALL = 64

    def p0a_dve():
        e = "dve"
        for tt in range(NT):
            em.op(e, lambda g, t=tt: g.reduce_sum(
                D["SUMS"][:, t:t + 1], D["XKB"][t % 2][:, :], axis=AX.X),
                waits=[dwait("xk", tt)],
                inc=("c_dve", 1, ("dst", tt)), tag=f"red{tt}")
        em.op(e, lambda g: g.tensor_scalar(
            D["MNEG"][:, :], D["SUMS"][:, :], -1.0 / 4096.0, None, ALU.mult),
            waits=[("c_dve", ("dst", NT - 1))],
            inc=("c_dve", 1, ("mneg",)), tag="mneg")
        em.op(e, lambda g: g.tensor_scalar(
            D["EX2"][:, :], D["SUMSQ"][:, :], 1.0 / 4096.0, None, ALU.mult),
            waits=[("c_act", ("ast", NT - 1))],
            inc=("c_dve", 1, ("ex2",)), tag="ex2")
        em.op(e, lambda g: g.tensor_tensor(
            D["MSQ"][:, :], D["MNEG"][:, :], D["MNEG"][:, :], ALU.mult),
            waits=[("c_dve", ("mneg",))],
            inc=("c_dve", 1, ("msq",)), tag="msq")
        em.op(e, lambda g: g.tensor_tensor(
            D["EX2"][:, :], D["EX2"][:, :], D["MSQ"][:, :], ALU.subtract),
            waits=[("c_dve", ("msq",)), ("c_dve", ("ex2",))],
            inc=("c_dve", 1, ("var",)), tag="var")
        em.op(e, lambda g: g.tensor_tensor(
            D["MR_OWN"][0:1, :], D["MRO_OWN"][0:1, :], D["RR_OWN"][0:1, :],
            ALU.mult),
            waits=[("c_act", ("strev", 15))],
            inc=("c_dve", 1, ("mrown",)), tag="mrown")
        for ht in range(KT):
            em.op(e, lambda g, h=ht: g.tensor_tensor(
                D["SCR"][:, 0:1024], D["XQTB"][h % 2][:, :], D["RBCO"][:, :],
                ALU.mult),
                waits=[dwait("xqt", ht), ("c_act", ("obc", 3)),
                       ("c_act", ("ast", NT - 1)),
                       ("c_dve", ("xnt", ht - 1))],
                inc=("c_dve", 1, ("xnm", ht)), tag=f"xnm{ht}")
            em.op(e, lambda g, h=ht: g.tensor_tensor(
                D["XNT"][:, ts(h, 1024)], D["SCR"][:, 0:1024],
                D["MRBCO"][:, :], ALU.add),
                waits=[("c_dve", ("xnm", ht))],
                inc=("c_dve", 1, ("xnt", ht)), tag=f"xns{ht}")

    def p0a_act():
        e = "act"
        for tt in range(NT):
            em.op(e, lambda g, t=tt: g.activation(
                D["SCR"][:, 0:4096], D["XKB"][t % 2][:, :], AF.Square,
                accum_out=D["SUMSQ"][:, t:t + 1]),
                waits=[dwait("xk", tt), ("c_act", ("ast", tt - 1))],
                inc=("c_act", 1, ("ast", tt)), tag=f"sq{tt}")
        em.op(e, lambda g: g.activation(D["LNV"][:, :], D["EX2"][:, :],
                                        AF.Ln, bias=D["EPSC"][:, 0:1]),
              waits=[("c_dve", ("var",)), ("d_msc", 48)],
              inc=("c_act", 1, ("lnv",)), tag="lnv")
        em.op(e, lambda g: g.activation(D["RSTD"][:, :], D["LNV"][:, :],
                                        AF.Exp, scale=-0.5),
              waits=[("c_act", ("lnv",))],
              inc=("c_act", 1, ("rstd",)), tag="rstd")
        for w in range(2):
            em.op(e, lambda g, ww=w: g.copy(D["CSB"][:, ww:ww + 1],
                                            D["CS"][:, ww:ww + 1]),
                  waits=[("c_pe", ("csum", w))],
                  inc=("c_act", 1, ("csev", w)), tag=f"csev{w}")
        for c in range(32, NT):
            for w in range(2):
                idx = 2 * (c - 32) + w
                dst = D["MRO_OWN"] if w == 0 else D["RR_OWN"]
                ofs = 128 * (c - 32)
                em.op(e, lambda g, d=dst, o=ofs: g.copy(
                    d[0:1, o:o + 128], D["TR"][0:1, :]),
                    waits=[("c_pe", ("str", idx))],
                    inc=("c_act", 1, ("strev", idx)), tag=f"strev{idx}")
        for w in range(2):
            dst = D["CKR"] if w == 0 else D["CVR"]
            em.op(e, lambda g, d=dst: g.copy(d[0:1, :], D["TR"][0:1, :]),
                  waits=[("c_pe", ("cstr", w))],
                  inc=("c_act", 1, ("cstrev", w)), tag=f"cstrev{w}")
        for i in range(4):
            dst = D["RBCO"] if i < 2 else D["MRBCO"]
            em.op(e, lambda g, d=dst, ii=i: g.copy(
                d[:, 512 * (ii % 2):512 * (ii % 2) + 512], D["RB"][:, :]),
                waits=[("c_pe", ("obc", i))],
                inc=("c_act", 1, ("obc", i)), tag=f"obcev{i}")

    def p0a_pe():
        e = "pe"
        for w in range(2):
            for kt in range(KT):
                em.op(e, lambda g, ww=w, k=kt: g.matmul(
                    D["CS"][:, ww:ww + 1],
                    D["WKV"][:, k * 256 + 128 * ww:k * 256 + 128 * ww + 128],
                    D["ON16"][:, :], start=(k == 0), stop=(k == KT - 1)),
                    waits=([("d_cst", CST_ALL)] if (w == 0 and kt == 0)
                           else ([("c_act", ("csev", 0))]
                                 if (w == 1 and kt == 0) else [])),
                    inc=(("c_pe", 1, ("csum", w)) if kt == KT - 1 else None),
                    tag=f"cs{w}_{kt}")
        for c in range(32, NT):
            for w in range(2):
                idx = 2 * (c - 32) + w
                srcT = D["MNEG"] if w == 0 else D["RSTD"]
                wl = [("c_act", ("strev", idx - 1))]
                if idx == 0:
                    wl += [("c_dve", ("var",)), ("d_msc", 48)]
                if w == 1:
                    wl.append(("c_act", ("rstd",)))
                em.op(e, lambda g, s=srcT, cc=c: g.transpose(
                    D["TR"][:, :], s[:, cc:cc + 1], D["ID32"][:, :]),
                    waits=wl, inc=("c_pe", 1, ("str", idx)), tag=f"str{idx}")
        for w in range(2):
            wl = [("c_act", ("csev", w)), ("c_act", ("strev", 15))]
            if w == 1:
                wl.append(("c_act", ("cstrev", 0)))
            em.op(e, lambda g, ww=w: g.transpose(
                D["TR"][:, :], D["CSB"][:, ww:ww + 1], D["ID32"][:, :]),
                waits=wl, inc=("c_pe", 1, ("cstr", w)), tag=f"cstr{w}")
        for i in range(4):
            row = "RR_OWN" if i < 2 else "MR_OWN"
            wl = [("c_act", ("cstrev", 1)), ("c_act", ("obc", i - 1))]
            if i == 2:
                wl.append(("c_dve", ("mrown",)))
            em.op(e, lambda g, r=row, ii=i: g.matmul(
                D["RB"][:, :], D["ONR"][:, :],
                D[r][0:1, 512 * (ii % 2):512 * (ii % 2) + 512],
                start=True, stop=True),
                waits=wl, inc=("c_pe", 1, ("obc", i)), tag=f"obc{i}")

    def p0b_gp():
        e = "gp"
        for ch in range(8):
            for ht in range(KT):
                em.op(e, lambda g, c=ch, h=ht: g.dma_start(
                    out=D["SLAB"][c % 2][:, ts(h, 512)],
                    in_=D["xkT"][h][:, 512 * c:512 * c + 512]),
                    waits=([("c_pe", ("kv", ch - 2))] if ht == 0 else []),
                    inc=(dinc("slab", ch) if ht == KT - 1
                         else (f"d_slab{ch % 2}", 16, None)),
                    tag=f"slab{ch}_{ht}")

    def p0b_pe():
        e = "pe"
        for ch in range(8):
            for l in range(4):
                for w in range(2):
                    kidx = ch * 8 + l * 2 + w
                    srcT = D["MNEG"] if w == 0 else D["RSTD"]
                    em.op(e, lambda g, s=srcT, cc=4 * ch + l: g.transpose(
                        D["TR"][:, :], s[:, cc:cc + 1], D["ID32"][:, :]),
                        waits=[("c_act", ("krowev", kidx - 1))],
                        inc=("c_pe", 1, ("krow", kidx)), tag=f"krow{kidx}")
            em.op(e, lambda g, c=ch: g.matmul(
                D["RB"][:, :], D["ONR"][:, :],
                D["RRB"][c % 2][0:1, :], start=True, stop=True),
                waits=[("c_act", ("rbev", ch - 1)),
                       ("c_dve", ("kvev", ch - 2)),
                       ("c_act", ("krowev", ch * 8 + 7))],
                inc=("c_pe", 1, ("rb", ch)), tag=f"rb{ch}")
            for kt in range(KT):
                wl = []
                if kt == 0:
                    wl = [dwait("slab", ch), ("c_dve", ("kvev", ch - 2))]
                em.op(e, lambda g, c=ch, k=kt: g.matmul(
                    D["KP"][c % 2][:, :], D["WKV"][:, k * 256:k * 256 + 128],
                    D["SLAB"][c % 2][:, ts(k, 512)],
                    start=(k == 0), stop=False),
                    waits=wl, tag=f"kvK{ch}_{kt}")
                em.op(e, lambda g, c=ch, k=kt: g.matmul(
                    D["VP"][c % 2][:, :],
                    D["WKV"][:, k * 256 + 128:k * 256 + 256],
                    D["SLAB"][c % 2][:, ts(k, 512)],
                    start=(k == 0), stop=False), tag=f"kvV{ch}_{kt}")
            em.op(e, lambda g, c=ch: g.matmul(
                D["KP"][c % 2][:, :], D["CKR"][0:1, :],
                D["MRB"][c % 2][0:1, :], start=False, stop=True),
                tag=f"kvKo{ch}")
            em.op(e, lambda g, c=ch: g.matmul(
                D["VP"][c % 2][:, :], D["CVR"][0:1, :],
                D["MRB"][c % 2][0:1, :], start=False, stop=True),
                inc=("c_pe", 1, ("kv", ch)), tag=f"kvVo{ch}")
            if ch >= 1:
                for l in range(4):
                    vi = (ch - 1) * 4 + l
                    em.op(e, lambda g, c=ch - 1, ll=l: g.transpose(
                        D["PVT"][:, :], D["VTTB"][c % 2][:, ts(ll, 128)],
                        D["ID16"][:, :]),
                        waits=[("c_dve", ("kvev", ch - 1)),
                               ("c_dve", ("vtev", vi - 1))],
                        inc=("c_pe", 1, ("vt", vi)), tag=f"vt{vi}")
        for l in range(4):
            vi = 7 * 4 + l
            em.op(e, lambda g, ll=l: g.transpose(
                D["PVT"][:, :], D["VTTB"][1][:, ts(ll, 128)], D["ID16"][:, :]),
                waits=[("c_dve", ("kvev", 7)), ("c_dve", ("vtev", vi - 1))],
                inc=("c_pe", 1, ("vt", vi)), tag=f"vt{vi}")

    def p0b_act():
        e = "act"
        for ch in range(8):
            for l in range(4):
                for w in range(2):
                    kidx = ch * 8 + l * 2 + w
                    dst = D["MRB"] if w == 0 else D["RRB"]
                    em.op(e, lambda g, d=dst, c=ch, ll=l: g.copy(
                        d[c % 2][0:1, 128 * ll:128 * ll + 128],
                        D["TR"][0:1, :]),
                        waits=[("c_pe", ("krow", kidx))],
                        inc=("c_act", 1, ("krowev", kidx)), tag=f"krowev{kidx}")
            em.op(e, lambda g, c=ch: g.copy(D["RBC"][c % 2][:, :],
                                            D["RB"][:, :]),
                  waits=[("c_pe", ("rb", ch))],
                  inc=("c_act", 1, ("rbev", ch)), tag=f"rbev{ch}")

    def p0b_dve():
        e = "dve"
        for ch in range(8):
            em.op(e, lambda g, c=ch: g.tensor_tensor(
                D["KTS"][:, 512 * c:512 * c + 512], D["KP"][c % 2][:, :],
                D["RBC"][c % 2][:, :], ALU.mult),
                waits=[("c_pe", ("kv", ch)), ("c_act", ("rbev", ch))],
                tag=f"kevK{ch}")
            wl2 = [("c_pe", ("vt", (ch - 2) * 4 + 3))] if ch >= 2 else []
            em.op(e, lambda g, c=ch: g.tensor_tensor(
                D["VTTB"][c % 2][:, :], D["VP"][c % 2][:, :],
                D["RBC"][c % 2][:, :], ALU.mult),
                waits=wl2, inc=("c_dve", 1, ("kvev", ch)), tag=f"kevV{ch}")
            if ch >= 1:
                for l in range(4):
                    vi = (ch - 1) * 4 + l
                    em.op(e, lambda g, k=vi: g.tensor_copy(
                        D["VSB"][:, ts(k, 128)], D["PVT"][:, :]),
                        waits=[("c_pe", ("vt", vi))],
                        inc=("c_dve", 1, ("vtev", vi)), tag=f"vtev{vi}")
        for vi in range(7 * 4, KT):
            em.op(e, lambda g, k=vi: g.tensor_copy(
                D["VSB"][:, ts(k, 128)], D["PVT"][:, :]),
                waits=[("c_pe", ("vt", vi))],
                inc=("c_dve", 1, ("vtev", vi)), tag=f"vtev{vi}")

    def wb_idx(hf, kind, i):
        base = hf * (NH + KT)
        return base + (i if kind == "q" else NH + i)

    def p1_gp2():
        e = "gp"
        for hf in range(2):
            for h in range(NH):
                idx = wb_idx(hf, "q", h)
                if h >= 2:
                    wl = [("c_pe", ("qmm", li(hf, h - 2)))]
                elif hf == 1:
                    wl = [("c_pe", ("womm", (0, KT - 2 + h)))]
                else:
                    wl = []
                em.op(e, lambda g, hh=h, b=idx % 2: g.dma_start(
                    out=D["WB"][b][:, :], in_=D["wq"][hh]),
                    waits=wl, inc=dinc("wb", idx), tag=f"wq{hf}_{h}")
            for ot in range(KT):
                idx = wb_idx(hf, "o", ot)
                if ot >= 2:
                    wl = [("c_pe", ("womm", (hf, ot - 2)))]
                else:
                    wl = [("c_pe", ("qmm", li(hf, NH - 2 + ot)))]
                em.op(e, lambda g, o=ot, b=idx % 2: g.dma_start(
                    out=D["WB"][b][:, :], in_=D["wo"][o]),
                    waits=wl, inc=dinc("wb", idx), tag=f"wo{hf}_{ot}")

    def p1_sp():
        e = "sp"
        for hf in range(2):
            for h in range(NH):
                L = li(hf, h)
                em.op(e, lambda g, hh=h, b=L % 2: g.dma_start(
                    out=D["BRELB"][b][:, :], in_=D["brel"][hh]),
                    waits=[("c_dve", ("emask", L - 2))],
                    inc=dinc("brel", L), tag=f"brel{hf}_{h}")
            for ot in range(KT):
                gidx = hf * KT + ot
                em.op(e, lambda g, f=hf, o=ot, b=gidx % 2: g.dma_start(
                    out=D["s_scr"][o][:, 512 * f:512 * f + 512],
                    in_=D["SSB"][b][:, :]),
                    waits=[("c_act", ("sev", (hf, ot)))],
                    inc=dinc("sst", gidx), tag=f"sst{hf}_{ot}")

    def p1_pe():
        e = "pe"
        for hf in range(2):
            for i in range(NH + 3):
                q = i
                if q < NH:
                    L = li(hf, q)
                    for kt in range(KT):
                        wl = []
                        if kt == 0:
                            wl = [dwait("wb", wb_idx(hf, "q", q)),
                                  ("c_act", ("qev", L - 1))]
                            if L == 0:
                                wl += [("c_dve", ("vtev", KT - 1)),
                                       ("c_dve", ("xnt", KT - 1))]
                        em.op(e, lambda g, f=hf, k=kt, b=wb_idx(hf, "q", q) % 2:
                              g.matmul(
                                  D["QPS"][:, :], D["WB"][b][:, ts(k, 128)],
                                  D["XNT"][:, k * 1024 + 512 * f:
                                           k * 1024 + 512 * f + 512],
                                  start=(k == 0), stop=(k == KT - 1)),
                              waits=wl,
                              inc=(("c_pe", 1, ("qmm", L))
                                   if kt == KT - 1 else None),
                              tag=f"q{hf}_{q}_{kt}")
                sH = i - 1
                if 0 <= sH < NH:
                    L = li(hf, sH)
                    for kt in range(KTN[hf]):
                        n = SP_N[hf][kt]
                        s0 = SP_S0[hf][kt]
                        wl = [("c_act", ("qev", L))] if kt == 0 else []
                        if kt >= 2:
                            wl.append(("c_act", ("exp", (L, kt - 2))))
                        else:
                            pk = ktn_of(L - 1) - 2 + (kt % 2)
                            wl.append(("c_act", ("exp", (L - 1, pk))))
                        em.op(e, lambda g, k=kt, nn=n, ss=s0, ll=L: g.matmul(
                            D["SC"][k % 2][:, 0:nn], D["KTS"][:, ts(k, 128)],
                            D["QH"][ll % 2][:, ss:512], start=True, stop=True),
                            waits=wl, inc=("c_pe", 1, ("sc", (L, kt))),
                            tag=f"sc{hf}_{sH}_{kt}")
                a = i - 2
                if 0 <= a < NH:
                    L = li(hf, a)
                    for kt in range(KTN[hf]):
                        n = SP_N[hf][kt]
                        s0 = SP_S0[hf][kt]
                        o = SP_OFF[hf][kt]
                        wl = []
                        if kt == 0:
                            wl = [("c_dve", ("emask", L)),
                                  ("c_dve", ("avev", L - 1)),
                                  ("c_act", ("dev", L - 1))]
                        em.op(e, lambda g, f=hf, k=kt, nn=n, ss=s0, oo=o, ll=L:
                              g.matmul(
                                  D["AVP"][:, ss:512], D["VSB"][:, ts(k, 128)],
                                  D["ES"][ll % 2][:, oo:oo + nn],
                                  start=(k == 0), stop=(k == KTN[f] - 1)),
                              waits=wl, tag=f"av{hf}_{a}_{kt}")
                        em.op(e, lambda g, f=hf, k=kt, nn=n, ss=s0, oo=o, ll=L:
                              g.matmul(
                                  D["DPS"][0:1, ss:512], D["ON16"][:, :],
                                  D["ES"][ll % 2][:, oo:oo + nn],
                                  start=(k == 0), stop=(k == KTN[f] - 1)),
                              inc=(("c_pe", 1, ("avd", L))
                                   if kt == KTN[hf] - 1 else None),
                              tag=f"dn{hf}_{a}_{kt}")
                o_ = i - 3
                if 0 <= o_ < NH:
                    L = li(hf, o_)
                    em.op(e, lambda g, ll=L: g.matmul(
                        D["DBP"][:, :], D["ONR"][:, :],
                        D["DIR"][ll % 2][0:1, :], start=True, stop=True),
                        waits=[("c_dve", ("recip", L)),
                               ("c_dve", ("div", L - 1))],
                        inc=("c_pe", 1, ("outer", L)), tag=f"ou{hf}_{o_}")
            for ot in range(KT):
                for kt in range(KT):
                    wl = []
                    if kt == 0:
                        wl = [dwait("wb", wb_idx(hf, "o", ot))]
                        if ot >= 2:
                            wl.append(("c_act", ("sev", (hf, ot - 2))))
                        elif hf == 1:
                            wl.append(("c_act", ("sev", (0, KT - 2 + ot))))
                        if ot == 0:
                            wl.append(("c_dve", ("div", li(hf, NH - 1))))
                    em.op(e, lambda g, o=ot, k=kt, b=wb_idx(hf, "o", ot) % 2:
                          g.matmul(
                              D["SWO"][o % 2][:, :], D["WB"][b][:, ts(k, 128)],
                              D["ATT"][:, ts(k, 512)],
                              start=(k == 0), stop=(k == KT - 1)),
                          waits=wl,
                          inc=(("c_pe", 1, ("womm", (hf, ot)))
                               if kt == KT - 1 else None),
                          tag=f"wo{hf}_{ot}_{kt}")

    def p1_act():
        e = "act"
        for hf in range(2):
            for i in range(NH + 3):
                q = i
                if q < NH:
                    L = li(hf, q)
                    wl = [("c_pe", ("qmm", L))]
                    if L >= 2:
                        wl.append(("c_pe", ("sc", (L - 2, ktn_of(L - 2) - 1))))
                    em.op(e, lambda g, ll=L: g.copy(
                        D["QH"][ll % 2][:, :], D["QPS"][:, :]),
                        waits=wl, inc=("c_act", 1, ("qev", L)),
                        tag=f"qev{hf}_{q}")
                sH = i - 1
                if 0 <= sH < NH:
                    L = li(hf, sH)
                    for kt in range(KTN[hf]):
                        o = SP_OFF[hf][kt]
                        lo = max(umin(kt), 4 * hf)
                        wl = [("c_pe", ("sc", (L, kt)))]
                        if kt == 0:
                            wl += [("c_pe", ("avd", L - 2)),
                                   dwait("brel", L)]
                        nch = 4 * hf + 4 - lo
                        for ui in range(nch):
                            u = lo + ui
                            pi = PAIR_IDX[(kt, u)]
                            em.op(e, lambda g, k=kt, oo=o, uu=ui, ll=L, p=pi:
                                  g.activation(
                                      D["ES"][ll % 2][:, oo + 128 * uu:
                                                      oo + 128 * uu + 128],
                                      D["SC"][k % 2][:, 128 * uu:
                                                     128 * uu + 128],
                                      AF.Exp,
                                      bias=D["BRELB"][ll % 2][:, p:p + 1],
                                      scale=SCALE),
                                  waits=(wl if ui == 0 else []),
                                  inc=(("c_act", 1, ("exp", (L, kt)))
                                       if ui == nch - 1 else None),
                                  tag=f"ex{hf}_{sH}_{kt}_{u}")
                d = i - 2
                if 0 <= d < NH:
                    L = li(hf, d)
                    em.op(e, lambda g, ll=L: g.copy(
                        D["DRO"][ll % 2][0:1, :], D["DPS"][0:1, :]),
                        waits=[("c_pe", ("avd", L)),
                               ("c_dve", ("recip", L - 2))],
                        inc=("c_act", 1, ("dev", L)), tag=f"dev{hf}_{d}")
            for ot in range(KT):
                gidx = hf * KT + ot
                em.op(e, lambda g, o=ot, b=gidx % 2: g.copy(
                    D["SSB"][b][:, :], D["SWO"][o % 2][:, :]),
                    waits=[("c_pe", ("womm", (hf, ot))),
                           dwait("sst", gidx - 2)],
                    inc=("c_act", 1, ("sev", (hf, ot))), tag=f"sev{hf}_{ot}")

    def p1_dve():
        e = "dve"
        for hf in range(2):
            for i in range(NH + 4):
                v = i - 4
                if 0 <= v < NH:
                    L = li(hf, v)
                    wl = [("c_pe", ("outer", L))]
                    if hf == 1 and v == 0:
                        wl.append(("c_pe", ("womm", (0, KT - 1))))
                    wl.append(("c_dve", ("avev", L)))
                    em.op(e, lambda g, ll=L, vv=v: g.tensor_tensor(
                        D["ATT"][:, ts(vv, 512)], D["AVR"][ll % 2][:, :],
                        D["DBP"][:, :], ALU.mult),
                        waits=wl, inc=("c_dve", 1, ("div", L)),
                        tag=f"dv{hf}_{v}")
                m = i - 1
                if 0 <= m < NH:
                    L = li(hf, m)
                    nmask = 0
                    for kt in range(KTN[hf]):
                        if umin(kt) >= 4 * hf:
                            o = SP_OFF[hf][kt]
                            nmask += 1
                            em.op(e, lambda g, ll=L, oo=o, m4=kt % 4:
                                  g.tensor_tensor(
                                      D["ES"][ll % 2][:, oo:oo + 128],
                                      D["ES"][ll % 2][:, oo:oo + 128],
                                      D["MSK"][:, ts(m4, 128)], ALU.mult),
                                  waits=[("c_act", ("exp", (L, kt)))],
                                  inc=(("c_dve", 1, ("emask", L))
                                       if nmask == 16 else None),
                                  tag=f"mk{hf}_{m}_{kt}")
                a = i - 2
                if 0 <= a < NH:
                    L = li(hf, a)
                    em.op(e, lambda g, ll=L: g.tensor_copy(
                        D["AVR"][ll % 2][:, :], D["AVP"][:, :]),
                        waits=[("c_pe", ("avd", L)), ("c_dve", ("div", L - 2))],
                        inc=("c_dve", 1, ("avev", L)), tag=f"ae{hf}_{a}")
                    em.op(e, lambda g, ll=L: g.reciprocal(
                        D["DIR"][ll % 2][0:1, :], D["DRO"][ll % 2][0:1, :]),
                        waits=[("c_act", ("dev", L)),
                               ("c_pe", ("outer", L - 2))],
                        inc=("c_dve", 1, ("recip", L)), tag=f"rc{hf}_{a}")

    def wli(sp, wi):
        return sp * 2 + wi

    def p2_gp():
        e = "gp"
        for sp in range(NSPE):
            for wi in range(2):
                wb = wli(sp, wi)
                svsrc = D["svg"] if wi == 0 else D["svu"]
                em.op(e, lambda g, sr=svsrc: g.dma_start(
                    out=D["SVB"][:, :], in_=sr),
                    waits=[("c_pe", ("cgenp", ((wb - 1) * KT + KT - 1) // 2))],
                    inc=dinc("sv2", wb), tag=f"sv{wb}")
                susrc = D["sug"] if wi == 0 else D["suu"]
                em.op(e, lambda g, s=sp, sr=susrc, b=wb % 2: g.dma_start(
                    out=D["SUB"][b][:, :],
                    in_=sr[:, 1024 * s:1024 * s + 1024]),
                    waits=[("c_pe", ("cgenp", ((wb - 2) * KT + KT - 1) // 2))],
                    inc=dinc("su", wb), tag=f"su{wb}")
                srcw = D["gw"] if wi == 0 else D["uw"]
                for lb4 in range(KT // 4):
                    lb = wb * (KT // 4) + lb4
                    wl = []
                    if lb >= 2:
                        wl = [("c_dve", ("weff4", lb - 2))]
                    em.op(e, lambda g, s=sp, k4=lb4, sr=srcw, b=lb % 3:
                          g.dma_start(
                              out=D["GWB"][b][:, :].rearrange(
                                  "p (k c) -> p k c", k=4),
                              in_=sr[s][4 * k4:4 * k4 + 4].rearrange(
                                  "k p c -> p k c")),
                          waits=wl, inc=(f"d_gwb{lb % 3}", 16, ("gwb", lb)),
                          tag=f"gw{lb}")

    def p2_mm(sp, wi, itl, kk, gi_weff):
        e = "pe"
        wl = []
        if kk == 0:
            if itl >= 2:
                if wi == 0 and P2MODE != 5:
                    wl.append(("c_act", ("silu", (sp, itl - 2))))
                elif wi == 1:
                    wl.append(("c_dve", ("dmul", (sp, itl - 2))))
            else:
                if wi == 0 and P2MODE != 5:
                    wl.append(("c_dve", ("dmul", (sp - 1, itl + ITL - 2))))
                elif wi == 1 and P2MODE != 5:
                    wl.append(("c_act", ("silu", (sp, itl + ITL - 2))))
        if gi_weff is not None:
            wl.append(("c_dve", ("weff4", gi_weff // 4)))
        elif kk == 0 and itl == 1:
            wl.append(("c_dve", ("weff4", (wli(sp, wi) * KT + KT - 1) // 4)))
        psum = D["APS"] if itl % 2 == 0 else D["BPS"]
        for half in range(2):
            em.op(e, lambda g, it=itl, k=kk, hh=half, p=psum: g.matmul(
                p[:, 512 * hh:512 * hh + 512],
                D["WEFF"][:, k * 1024 + 128 * it:k * 1024 + 128 * it + 128],
                D["XNT"][:, k * 1024 + 512 * hh:k * 1024 + 512 * hh + 512],
                start=(k == 0), stop=(k == KT - 1)),
                waits=(wl if half == 0 else []),
                inc=(("c_pe", 1, (("gmm" if wi == 0 else "umm"), (sp, itl)))
                     if (kk == KT - 1 and half == 1) else None),
                tag=f"mm{wi}_{sp}_{itl}_{kk}")

    def p2_mm_half(sp, wi, itl, kk, gi_weff, hh):
        e = "pe"
        wl = []
        if gi_weff is not None:
            wl.append(("c_dve", ("weff", gi_weff)))
        elif kk == 0:
            wl.append(("c_dve", ("weff", wli(sp, wi) * KT + KT - 1)))
        psum = D["APS"] if wi == 0 else D["BPS"]
        em.op(e, lambda g, it=itl, k=kk, p=psum: g.matmul(
            p[:, 512 * hh:512 * hh + 512],
            D["WEFF"][:, k * 1024 + 128 * it:k * 1024 + 128 * it + 128],
            D["XNT"][:, k * 1024 + 512 * hh:k * 1024 + 512 * hh + 512],
            start=(k == 0), stop=(k == KT - 1)),
            waits=wl,
            inc=(("c_pe", 1, (("gmm" if wi == 0 else "umm"), (sp, itl)))
                 if (kk == KT - 1 and hh == 1) else None),
            tag=f"mmh{wi}_{sp}_{itl}_{kk}_{hh}")

    def p2_pe():
        e = "pe"
        for sp in range(NSPE):
            for wi in range(2):
                wb = wli(sp, wi)
                sv = D["SVB"]
                NOIL = int(os.environ.get("KNOIL", "0"))
                for k in range(KT + 6 if not NOIL else KT):
                    if k < KT:
                        gi = wb * KT + k
                        wl = [("c_act", ("c16p", (gi - 2) // 2))]
                        if k == 0:
                            wl += [dwait("su", wb), dwait("sv2", wb)]
                        for half in range(2):
                            em.op(e, lambda g, kk=k, hh=half, svv=sv,
                                  b=wb % 2: g.matmul(
                                      D["CPS"][kk % 2][:, 512 * hh:
                                                       512 * hh + 512],
                                      svv[:, ts(kk, 128)],
                                      D["SUB"][b][:, 512 * hh:512 * hh + 512],
                                      start=True, stop=True),
                                  waits=(wl if half == 0 else []),
                                  inc=(("c_pe", 1, ("cgenp", gi // 2))
                                       if (half == 1 and gi % 2 == 1)
                                       else None),
                                  tag=f"cg{wb}_{k}")
                    if (not NOIL) and k >= 6 and P2MODE >= (2 if wi == 0 else 6):
                        p2_mm(sp, wi, 0, k - 6, wb * KT + (k - 6))
                if P2MODE >= (2 if wi == 0 else 6):
                    if NOIL:
                        if int(os.environ.get("KSEQMM", "0")):
                            for hh in range(2):
                                for kk in range(KT):
                                    p2_mm_half(sp, wi, 0, kk,
                                               wb * KT + kk if hh == 0 else None,
                                               hh)
                        else:
                            for kk in range(KT):
                                p2_mm(sp, wi, 0, kk, wb * KT + kk)
                    for itl in range(1, ITLE):
                        for kk in range(KT):
                            p2_mm(sp, wi, itl, kk, None)


    def p2_act():
        e = "act"
        for sp in range(NSPE):
            for wi in range(2):
                wb = wli(sp, wi)
                for kt in range(KT):
                    gi = wb * KT + kt
                    wl = [("c_pe", ("cgenp", gi // 2))]
                    if gi >= 6 and os.environ.get("KC16WAR", "1") == "1":
                        wl.append(("c_dve", ("weff4", (gi - 6) // 4)))
                    em.op(e, lambda g, k=kt, b=gi % 6: g.copy(
                        D["C16"][b][:, :], D["CPS"][k % 2][:, :]),
                        waits=wl,
                        inc=(("c_act", 1, ("c16p", gi // 2))
                             if gi % 2 == 1 else None), tag=f"c16_{gi}")
                if wi == 0 and (P2MODE >= 3 or P2MODE == 7):
                    sfunc = os.environ.get("KSILU", "Silu")
                    def _sil(g, it, ff=sfunc):
                        if ff == "helpercopy":
                            return g.copy(D["SILA"][:, ts(it, 1024)],
                                          D["APS"][:, :])
                        if ff == "helpercps":
                            return g.copy(D["SILA"][:, ts(it, 1024)],
                                          D["CPS"][0][:, :])
                        if ff == "c16out":
                            return g.activation(D["C16"][0][:, :],
                                                D["APS"][:, :], AF.Silu)
                        if ff == "half0":
                            return g.copy(D["SILA"][:, 0:512],
                                          D["APS"][:, 0:512])
                        if ff == "half1":
                            return g.copy(D["SILA"][:, 512:1024],
                                          D["APS"][:, 512:1024])
                        if ff == "bpsread":
                            return g.copy(D["SILA"][:, 0:512],
                                          D["BPS"][:, 0:512])
                        if ff == "twohalf":
                            g.copy(D["SILA"][:, 0:512], D["APS"][:, 0:512])
                            return g.copy(D["SILA"][:, 512:1024],
                                          D["APS"][:, 512:1024])
                        return g.activation(
                            D["SILA"][:, ts(it, 1024)],
                            D["APS" if it % 2 == 0 else "BPS"][:, :],
                            getattr(AF, ff))
                    for itl in range(ITLE):
                        em.op(e, lambda g, s=sp, it=itl: _sil(g, it),
                            waits=[("c_pe", ("gmm", (sp, itl)))],
                            inc=("c_act", 1, ("silu", (sp, itl))),
                            tag=f"sil{sp}_{itl}")

    def p2_dve():
        e = "dve"
        for sp in range(NSPE):
            for wi in range(2):
                wb = wli(sp, wi)
                for kt in range(KT):
                    gi = wb * KT + kt
                    lb = gi // 4
                    em.op(e, lambda g, k=kt, b=lb % 3, b2=gi % 6, q=gi % 4:
                          g.tensor_tensor(
                              D["WEFF"][:, ts(k, 1024)],
                              D["GWB"][b][:, 1024 * q:1024 * q + 1024],
                              D["C16"][b2][:, :], ALU.mult),
                          waits=[("c_act", ("c16p", gi // 2)),
                                 (f"d_gwb{lb % 3}", ("gwb", lb)),
                                 ("c_pe", (("gmm" if (wb - 1) % 2 == 0
                                            else "umm"),
                                           ((wb - 1) // 2, ITL - 1)))],
                          inc=(("c_dve", 1, ("weff4", gi // 4))
                               if gi % 4 == 3 else None), tag=f"wf{gi}")
                if wi == 1 and P2MODE >= 6:
                    for itl in range(ITL):
                        it_g = sp * ITL + itl
                        em.op(e, lambda g, s=sp, it=itl, b=it_g % 2:
                              g.tensor_tensor(
                                  D["DST"][b][:, :], D["SILA"][:, ts(it, 1024)],
                                  D["APS" if it % 2 == 0 else "BPS"][:, :],
                                  ALU.mult),
                              waits=[("c_pe", ("umm", (sp, itl))),
                                     ("c_act", ("silu", (sp, itl))),
                                     dwait("dstore", it_g - 2)],
                              inc=("c_dve", 1, ("dmul", (sp, itl))),
                              tag=f"dm{sp}_{itl}")

    def p2_sp():
        e = "sp"
        if P2MODE < 6:
            return
        for sp in range(NSPE):
            for itl in range(ITL):
                it_g = sp * ITL + itl
                em.op(e, lambda g, ig=it_g, b=it_g % 2: g.dma_start(
                    out=D["d_scr"][ig // 4, :, ig % 4, :],
                    in_=D["DST"][b][:, :]),
                    waits=[("c_dve", ("dmul", (sp, itl)))],
                    inc=dinc("dstore", it_g), tag=f"dsp{it_g}")

    def p3_gp():
        e = "gp"
        em.op(e, lambda g: g.dma_start(out=D["SUDB"][:, :], in_=D["sud"]),
              inc=("d_sv3", 16, None), tag="sudl")
        def svd_load(oq, cc):
            lc = oq * 8 + cc
            if cc >= 2:
                wl = [("c_pe", ("cgdp",
                                (oq * NIT + 14 * (cc - 2) + 13) // 2))]
            elif oq >= 1:
                wl = [("c_pe", ("cgdp", ((oq - 1) * NIT + NIT - 1) // 2))]
            else:
                wl = []
            em.op(e, lambda g, c=cc, b=lc % 2: g.dma_start(
                out=D["SVDB"][b][:, :],
                in_=D["svd"][:, 1792 * c:1792 * c + 1792]),
                waits=wl, inc=dinc("svd", lc), tag=f"svd{lc}")

        for oq in range(NOQE):
            svd_load(oq, 0)
            svd_load(oq, 1)
            for lb4 in range(NIT // 4):
                lb = oq * (NIT // 4) + lb4
                wl = []
                if lb >= 2:
                    wl = [("c_dve", ("weffd4", lb - 2))]
                em.op(e, lambda g, o=oq, k4=lb4, b=lb % 3: g.dma_start(
                    out=D["DWB"][b][:, :].rearrange(
                        "p (k c) -> p k c", k=4),
                    in_=D["dw"][o][4 * k4:4 * k4 + 4].rearrange(
                        "k p c -> p k c")),
                    waits=wl, inc=(f"d_dwb{lb % 3}", 16, ("dwb", lb)),
                    tag=f"dw{lb}")
                if lb4 >= 4 and lb4 % 4 == 0 and lb4 // 4 + 1 <= 7:
                    svd_load(oq, lb4 // 4 + 1)
            for t in range(4):
                og = oq * 4 + t
                em.op(e, lambda g, gg=og, b=og % 2: g.dma_start(
                    out=D["outT"][gg], in_=D["OUTB"][b][:, :]),
                    waits=[("c_dve", ("comb", og))],
                    inc=dinc("ost", og), tag=f"ost{og}")
        em.op(e, None, waits=[dwait("ost", NOQ * 4 - 2),
                              dwait("ost", NOQ * 4 - 1)], tag="finwait")

    def p3_sp():
        e = "sp"
        for oq in range(NOQE):
            for lb4 in range(NIT // 4):
                lb = oq * (NIT // 4) + lb4
                if lb4 >= 2:
                    wl = [("c_pe", ("dnm", (oq, 4 * (lb4 - 2) + 3)))]
                elif oq >= 1:
                    wl = [("c_pe", ("dnm", (oq - 1, NIT - 8 + 4 * lb4 + 3)))]
                else:
                    wl = []
                if lb == 0:
                    wl += [dwait("dstore", NIT - 2), dwait("dstore", NIT - 1)]
                em.op(e, lambda g, k4=lb4, b=lb % 3: g.dma_start(
                    out=D["DTB"][b][:, :],
                    in_=D["d_scr"][k4].rearrange("p t c -> p (t c)")),
                    waits=wl, inc=(f"d_dtb{lb % 3}", 16, ("dtb", lb)),
                    tag=f"dt{lb}")
            for t in range(4):
                og = oq * 4 + t
                wl = [("c_dve", ("comb", og - 2))]
                if og == 0:
                    wl += [dwait("sst", 2 * KT - 2), dwait("sst", 2 * KT - 1)]
                em.op(e, lambda g, o=og, b=og % 2: g.dma_start(
                    out=D["SLB"][b][:, :], in_=D["s_scr"][o]),
                    waits=wl, inc=dinc("sl", og), tag=f"sl{og}")
                em.op(e, lambda g, o=og, b=og % 2: g.dma_start(
                    out=D["XQB"][b][:, :], in_=D["xqT"][o]),
                    inc=dinc("xq2", og), tag=f"xq2{og}")

    def p3_pe():
        e = "pe"
        for oq in range(NOQE):
            for ik in range(NIT):
                gi = oq * NIT + ik
                wl = [("c_act", ("c16dp", (gi - 2) // 2)),
                      dwait("svd", oq * 8 + ik // 14)]
                if ik == 0:
                    wl.append(("c_act", ("gsil", (oq - 1, 0))))
                    wl.append(("d_sv3", 16))
                em.op(e, lambda g, o=oq, k=ik, b=gi % 2: g.matmul(
                    D["OUT"][0][:, 512 * b:512 * b + 512],
                    D["SVDB"][(oq * 8 + k // 14) % 2][:, ts(k % 14, 128)],
                    D["SUDB"][:, 512 * o:512 * o + 512],
                    start=True, stop=True),
                    waits=wl, inc=(("c_pe", 1, ("cgdp", gi // 2))
                                   if gi % 2 == 1 else None),
                    tag=f"cgd{gi}")
            for ik in range(NIT):
                gi = oq * NIT + ik
                wl = [("c_dve", ("weffd4", gi // 4)),
                      (f"d_dtb{(gi // 4) % 3}", ("dtb", gi // 4))]
                if ik == 0:
                    wl.append(("c_act", ("c16dp", (oq * NIT + NIT - 1) // 2)))
                    for t in range(4):
                        wl.append(("c_act", ("gsil", (oq - 1, t))))
                for t in range(4):
                    for half in range(2):
                        em.op(e, lambda g, k=ik, tt=t, hh=half,
                              b=(gi // 4) % 3, q=gi % 4: g.matmul(
                                  D["OUT"][tt][:, 512 * hh:512 * hh + 512],
                                  D["WEFD"][:, k * 512 + 128 * tt:
                                            k * 512 + 128 * tt + 128],
                                  D["DTB"][b][:, 1024 * q + 512 * hh:
                                              1024 * q + 512 * hh + 512],
                                  start=(k == 0), stop=(k == NIT - 1)),
                              waits=(wl if (t == 0 and half == 0) else []),
                              inc=(("c_pe", 1, ("dnm", (oq, ik)))
                                   if (t == 3 and half == 1) else None),
                              tag=f"dnm{oq}_{ik}_{t}")

    def p3_act():
        e = "act"
        for oq in range(NOQE):
            for ik in range(NIT):
                gi = oq * NIT + ik
                wl = [("c_pe", ("cgdp", gi // 2))]
                if gi >= 6:
                    wl.append(("c_dve", ("weffd4", (gi - 6) // 4)))
                em.op(e, lambda g, b=gi % 6, b2=gi % 2: g.copy(
                    D["C16D"][b][:, :],
                    D["OUT"][0][:, 512 * b2:512 * b2 + 512]),
                    waits=wl,
                    inc=(("c_act", 1, ("c16dp", gi // 2))
                         if gi % 2 == 1 else None), tag=f"c16d{gi}")
            for t in range(4):
                og = oq * 4 + t
                em.op(e, lambda g, o=oq, tt=t, b=og % 2: g.activation(
                    D["GB"][b][:, :], D["OUT"][tt][:, :], AF.Silu),
                    waits=[("c_pe", ("dnm", (oq, NIT - 1))),
                           ("c_dve", ("comb", og - 2))],
                    inc=("c_act", 1, ("gsil", (oq, t))), tag=f"gs{og}")

    def p3_dve():
        e = "dve"
        for oq in range(NOQE):
            for ik in range(NIT):
                gi = oq * NIT + ik
                lb = gi // 4
                em.op(e, lambda g, k=ik, b=lb % 3, b2=gi % 6, q=gi % 4:
                      g.tensor_tensor(
                          D["WEFD"][:, k * 512:(k + 1) * 512],
                          D["DWB"][b][:, 512 * q:512 * q + 512],
                          D["C16D"][b2][:, :], ALU.mult),
                      waits=[("c_act", ("c16dp", gi // 2)),
                             (f"d_dwb{lb % 3}", ("dwb", lb)),
                             ("c_pe", ("dnm", (oq - 1, NIT - 1)))],
                      inc=(("c_dve", 1, ("weffd4", gi // 4))
                           if gi % 4 == 3 else None), tag=f"wfd{gi}")
            for t in range(4):
                og = oq * 4 + t
                em.op(e, lambda g, o=oq, tt=t, b=og % 2: g.tensor_tensor(
                    D["SGB"][:, :], D["GB"][b][:, :], D["SLB"][b][:, :],
                    ALU.mult),
                    waits=[("c_act", ("gsil", (oq, t))), dwait("sl", og),
                           dwait("xq2", og), dwait("ost", og - 2)],
                    inc=("c_dve", 1, ("sg", og)), tag=f"sg{og}")
                em.op(e, lambda g, b=og % 2: g.tensor_tensor(
                    D["OUTB"][b][:, :], D["SGB"][:, :], D["XQB"][b][:, :],
                    ALU.add),
                    waits=[("c_dve", ("sg", og))],
                    inc=("c_dve", 1, ("comb", og)), tag=f"cb{og}")

    phases = [
        (1, {"sp": p0a_sp, "gp": p0a_gp, "pe": p0a_pe, "act": p0a_act,
             "dve": p0a_dve}),
        (2, {"gp": p0b_gp, "pe": p0b_pe, "act": p0b_act, "dve": p0b_dve}),
        (3, {"gp": p1_gp2, "sp": p1_sp, "pe": p1_pe, "act": p1_act,
             "dve": p1_dve}),
        (4, {"gp": p2_gp, "sp": p2_sp, "pe": p2_pe, "act": p2_act,
             "dve": p2_dve}),
        (5, {"gp": p3_gp, "sp": p3_sp, "pe": p3_pe, "act": p3_act,
             "dve": p3_dve}),
    ]
    phlim = int(os.environ.get("KPHASES", "5"))
    phskip = int(os.environ.get("KPHSKIP", "0"))
    if os.environ.get("KSIMINIT", "0") == "1":
        for tname in ("XNT", "KTS", "VSB", "MSK"):
            em.op("gp", lambda g, t=tname: g.memset(D[t][:, :], 0.25),
                  inc=("d_init", 1, None), tag=f"init{tname}")
        for e in ENGS:
            em.op(e, None, waits=[("d_init", 4)], tag="initw")
    for ph, emap in phases:
        if ph > phlim:
            break
        if ph <= phskip:
            BAR0[0] = 5 * ph
            continue
        for e in ENGS:
            bar_wait(e, ph - 1)
            if e in emap:
                emap[e]()
            bar_inc(e, ph)


def build_program():
    nc = bass.Bass("TRN2", target_bir_lowering=False,
                   detect_race_conditions=(os.environ.get("KRACE", "0") == "1"))
    D = {}

    def dram_in(name, shape):
        D[name] = nc.dram_tensor(name, list(shape), F32,
                                 kind="ExternalInput").ap()

    dram_in("xk", (32, 128, 4096))
    dram_in("xq", (8, 128, 4096))
    dram_in("xkT", (32, 128, 4096))
    dram_in("xqT", (32, 128, 1024))
    dram_in("wq", (32, 128, 4096))
    dram_in("wkv", (128, 8192))
    dram_in("wo", (32, 128, 4096))
    dram_in("gw", (14, 32, 128, 1024))
    dram_in("uw", (14, 32, 128, 1024))
    dram_in("dw", (8, 112, 128, 512))
    dram_in("sug", (128, 14336))
    dram_in("svg", (128, 4096))
    dram_in("suu", (128, 14336))
    dram_in("svu", (128, 4096))
    dram_in("sud", (128, 4096))
    dram_in("svd", (128, 14336))
    dram_in("masks", (128, 512))
    dram_in("brel", (32, 128, NPAIR))
    dram_in("ident", (128, 128))
    dram_in("ones_col", (128, 1))
    dram_in("ones_row", (1, 128))
    dram_in("epsc", (128, 1))
    D["d_scr"] = nc.dram_tensor("d_scr", [28, 128, 4, 1024], BF16,
                                kind="Internal").ap()
    D["s_scr"] = nc.dram_tensor("s_scr", [32, 128, 1024], F32,
                                kind="Internal").ap()
    D["outT"] = nc.dram_tensor("outT", [32, 128, 1024], F32,
                               kind="ExternalOutput").ap()

    def sb(st, name, shape, dt):
        D[name] = st.enter_context(nc.sbuf_tensor(name, list(shape), dt))

    def sbl(st, name, n, shape, dt):
        D[name] = [st.enter_context(
            nc.sbuf_tensor(f"{name}{i}", list(shape), dt)) for i in range(n)]

    def ps(st, name, shape, dt=F32):
        D[name] = st.enter_context(nc.psum_tensor(name, list(shape), dt))

    def psl(st, name, n, shape, dt=F32):
        D[name] = [st.enter_context(
            nc.psum_tensor(f"{name}{i}", list(shape), dt)) for i in range(n)]

    top = contextlib.ExitStack()
    sb(top, "ID32", (128, 128), F32)
    sb(top, "ID16", (128, 128), BF16)
    sb(top, "ONR", (1, 128), F32)
    sb(top, "ON16", (128, 1), BF16)
    sb(top, "EPSC", (128, 1), F32)
    sb(top, "CKR", (1, 128), F32)
    sb(top, "CVR", (1, 128), F32)
    sb(top, "CSB", (128, 2), F32)

    s_xnt = contextlib.ExitStack()
    sb(s_xnt, "XNT", (128, KT * 1024), BF16)

    s_p01 = contextlib.ExitStack()
    sb(s_p01, "KTS", (128, 4096), BF16)
    sb(s_p01, "VSB", (128, 4096), BF16)
    sbl(s_p01, "WB", 2, (128, 4096), BF16)
    sb(s_p01, "MSK", (128, 512), BF16)

    s_p0 = contextlib.ExitStack()
    sb(s_p0, "WKV", (128, 8192), BF16)
    sb(s_p0, "MNEG", (128, 40), F32)
    sb(s_p0, "RSTD", (128, 40), F32)
    sbl(s_p0, "MRB", 2, (1, 512), F32)
    sbl(s_p0, "RRB", 2, (1, 512), F32)
    sb(s_p0, "MRO_OWN", (1, 1024), F32)
    sb(s_p0, "RR_OWN", (1, 1024), F32)
    sb(s_p0, "MR_OWN", (1, 1024), F32)
    s_p0a = contextlib.ExitStack()
    sb(s_p0a, "XKB0", (128, 4096), F32)
    sb(s_p0a, "XKB1", (128, 4096), F32)
    D["XKB"] = [D["XKB0"], D["XKB1"]]
    sbl(s_p0a, "XQTB", 2, (128, 1024), F32)
    sb(s_p0a, "SCR", (128, 4096), F32)
    sb(s_p0a, "SUMS", (128, 40), F32)
    sb(s_p0a, "SUMSQ", (128, 40), F32)
    sb(s_p0a, "EX2", (128, 40), F32)
    sb(s_p0a, "MSQ", (128, 40), F32)
    sb(s_p0a, "LNV", (128, 40), F32)
    sb(s_p0a, "RBCO", (128, 1024), F32)
    sb(s_p0a, "MRBCO", (128, 1024), F32)
    s_p0a.close()
    s_p0b = contextlib.ExitStack()
    sbl(s_p0b, "SLAB", 2, (128, KT * 512), BF16)
    sbl(s_p0b, "RBC", 2, (128, 512), F32)
    sbl(s_p0b, "VTTB", 2, (128, 512), BF16)
    s_p0b.close()
    psl(s_p0, "KP", 2, (128, 512))
    psl(s_p0, "VP", 2, (128, 512))
    ps(s_p0, "CS", (128, 2))
    ps(s_p0, "TR", (1, 128))
    ps(s_p0, "RB", (128, 512))
    ps(s_p0, "PVT", (128, 128), BF16)
    s_p0.close()

    s_p1 = contextlib.ExitStack()
    sbl(s_p1, "ES", 2, (128, ECOLS), BF16)
    sb(s_p1, "ATT", (128, NH * 512), BF16)
    sbl(s_p1, "QH", 2, (128, 512), BF16)
    sbl(s_p1, "BRELB", 2, (128, NPAIR), F32)
    sbl(s_p1, "AVR", 2, (128, 512), F32)
    sbl(s_p1, "DRO", 2, (1, 512), F32)
    sbl(s_p1, "DIR", 2, (1, 512), F32)
    sbl(s_p1, "SSB", 2, (128, 512), F32)
    ps(s_p1, "QPS", (128, 512))
    psl(s_p1, "SC", 2, (128, 512))
    ps(s_p1, "AVP", (128, 512))
    ps(s_p1, "DPS", (1, 512))
    ps(s_p1, "DBP", (128, 512))
    psl(s_p1, "SWO", 2, (128, 512))
    s_p1.close()
    s_p01.close()

    s_p2 = contextlib.ExitStack()
    sb(s_p2, "WEFF", (128, KT * 1024), BF16)
    sb(s_p2, "SILA", (128, ITL * 1024), BF16)
    sbl(s_p2, "GWB", 3, (128, 4096), BF16)
    sbl(s_p2, "C16", 6, (128, 1024), BF16)
    sbl(s_p2, "DST", 2, (128, 1024), BF16)
    sb(s_p2, "SVB", (128, 4096), BF16)
    sbl(s_p2, "SUB", 2, (128, 1024), BF16)
    psl(s_p2, "CPS", 2, (128, 1024))
    ps(s_p2, "BPS", (128, 1024))
    ps(s_p2, "APS", (128, 1024))
    s_p2.close()
    s_xnt.close()

    s_p3 = contextlib.ExitStack()
    sb(s_p3, "WEFD", (128, NIT * 512), BF16)
    sb(s_p3, "SUDB", (128, 4096), BF16)
    sbl(s_p3, "SVDB", 2, (128, 1792), BF16)
    sbl(s_p3, "DWB", 3, (128, 2048), BF16)
    sbl(s_p3, "C16D", 6, (128, 512), BF16)
    sbl(s_p3, "DTB", 3, (128, 4096), BF16)
    sbl(s_p3, "SLB", 2, (128, 1024), F32)
    sbl(s_p3, "XQB", 2, (128, 1024), F32)
    sbl(s_p3, "GB", 2, (128, 1024), F32)
    sb(s_p3, "SGB", (128, 1024), F32)
    sbl(s_p3, "OUTB", 2, (128, 1024), F32)
    psl(s_p3, "OUT", 4, (128, 1024))
    s_p3.close()
    top.close()

    em = Em()
    build_phases(em, D)
    em.check()

    sems = contextlib.ExitStack()
    for s in sorted(em.count.keys()):
        em.semh[s] = sems.enter_context(nc.semaphore(s))
    em.emit = True
    with nc.Block() as block:
        @block.sync
        def _(eng):
            em.eng["sp"] = eng
            em.only = "sp"
            build_phases(em, D)

        @block.gpsimd
        def _(eng):
            em.eng["gp"] = eng
            em.only = "gp"
            build_phases(em, D)

        @block.tensor
        def _(eng):
            em.eng["pe"] = eng
            em.only = "pe"
            build_phases(em, D)

        @block.scalar
        def _(eng):
            em.eng["act"] = eng
            em.only = "act"
            build_phases(em, D)

        @block.vector
        def _(eng):
            em.eng["dve"] = eng
            em.only = "dve"
            build_phases(em, D)
    sems.close()
    return nc


_slopes = (2.0 ** (-2.0 ** (-(np.log2(NH) - 3)))) ** (np.arange(NH) + 1)
_slopes = _slopes.astype(np.float64)


def _pad128(a):
    out = np.zeros((128, a.shape[1]), dtype=np.float32)
    out[:a.shape[0]] = a
    return np.ascontiguousarray(out)


def _host_prep(inp):
    f32 = np.float32
    x = np.ascontiguousarray(inp["L0_attn_hidden_states"], dtype=f32)
    shared = {
        "wq": np.ascontiguousarray(
            np.asarray(inp["L0_attn_Wq"], f32).reshape(32, 128, 32, 128)
            .transpose(2, 1, 0, 3).reshape(32, 128, 4096)),
        "wkv": np.ascontiguousarray(
            np.asarray(inp["L0_attn_Wkv"], f32).reshape(32, 128, 256)
            .transpose(1, 0, 2).reshape(128, 8192)),
        "wo": np.ascontiguousarray(
            np.asarray(inp["L0_attn_Wo"], f32).reshape(32, 128, 32, 128)
            .transpose(2, 1, 0, 3).reshape(32, 128, 4096)),
        "gw": np.ascontiguousarray(
            np.asarray(inp["L0_ffn_gate_w"], f32).reshape(14, 1024, 32, 128)
            .transpose(0, 2, 3, 1)),
        "uw": np.ascontiguousarray(
            np.asarray(inp["L0_ffn_up_w"], f32).reshape(14, 1024, 32, 128)
            .transpose(0, 2, 3, 1)),
        "dw": np.ascontiguousarray(
            np.asarray(inp["L0_ffn_down_w"], f32).reshape(8, 512, 112, 128)
            .transpose(0, 2, 3, 1)),
        "sug": _pad128(np.asarray(inp["L0_ffn_gate_su"], f32).T),
        "svg": _pad128(np.asarray(inp["L0_ffn_gate_sv"], f32)),
        "suu": _pad128(np.asarray(inp["L0_ffn_up_su"], f32).T),
        "svu": _pad128(np.asarray(inp["L0_ffn_up_sv"], f32)),
        "sud": _pad128(np.asarray(inp["L0_ffn_down_su"], f32).T),
        "svd": _pad128(np.asarray(inp["L0_ffn_down_sv"], f32)),
        "ident": np.eye(128, dtype=f32),
        "ones_col": np.ones((128, 1), dtype=f32),
        "ones_row": np.ones((1, 128), dtype=f32),
        "epsc": np.full((128, 1), EPS, dtype=f32),
    }
    in_maps = []
    batch_cache = {}
    kl = np.arange(128)[:, None]
    ql = np.arange(128)[None, :]
    karr = np.arange(128, dtype=np.float64)
    for c in range(8):
        b, j = c // 4, c % 4
        if b not in batch_cache:
            xb = x[b]
            xbT = np.ascontiguousarray(xb.T)
            batch_cache[b] = (
                xb, xbT,
                np.ascontiguousarray(xb.reshape(32, 128, 4096)),
                np.ascontiguousarray(xbT.reshape(32, 128, 4096)))
        xb, xbT, xk_t, xkT_t = batch_cache[b]
        tiles = [4 * u + j for u in range(8)]
        xq = np.ascontiguousarray(
            np.stack([xb[128 * T:128 * T + 128] for T in tiles]))
        xqT = np.ascontiguousarray(
            np.concatenate([xbT[:, 128 * T:128 * T + 128] for T in tiles],
                           axis=1).reshape(32, 128, 1024))
        masks = np.zeros((128, 512), dtype=f32)
        for m in range(4):
            masks[:, 128 * m:128 * m + 128] = (
                (128 * m + kl) <= (128 * j + ql)).astype(f32)
        brel = np.zeros((32, 128, NPAIR), dtype=f32)
        for kt in range(KT):
            for u in range(umin(kt), 8):
                pi = PAIR_IDX[(kt, u)]
                K0 = 128.0 * (4 * u + j) + 64.0
                rel = (128.0 * kt + karr) - K0
                if u == umin(kt) and (kt % 4) > j:
                    brel[:, :, pi] = -30000.0
                else:
                    brel[:, :, pi] = (_slopes[:, None] *
                                      rel[None, :]).astype(f32)
        im = dict(shared)
        im.update({"xk": xk_t, "xq": xq, "xkT": xkT_t, "xqT": xqT,
                   "masks": masks, "brel": np.ascontiguousarray(brel)})
        in_maps.append(im)
    return in_maps


def kernel(**inputs):
    in_maps = _host_prep(inputs)
    nc = build_program()
    res = run_bass_kernel_spmd(nc, in_maps, core_ids=list(range(8)))
    out = np.empty((B, S, H), dtype=np.float32)
    for c in range(8):
        b, j = c // 4, c % 4
        oT = np.asarray(res.results[c]["outT"], dtype=np.float32)
        O = oT.transpose(2, 0, 1).reshape(1024, 4096)
        for u in range(8):
            T = 4 * u + j
            out[b, 128 * T:128 * T + 128, :] = O[128 * u:128 * u + 128]
    return out
